# version 10
# baseline (speedup 1.0000x reference)
import math
from contextlib import ExitStack

import numpy as np
import ml_dtypes

import concourse.bass as bass
import concourse.mybir as mybir
from concourse.bass_utils import run_bass_kernel_spmd

F32 = mybir.dt.float32
BF16 = mybir.dt.bfloat16
AF = mybir.ActivationFunctionType
ALU = mybir.AluOpType
AX = mybir.AxisListType

B, T, D = 8, 4096, 2048
H, DH = 16, 128
E, F, R, NL = 8, 2048, 64, 2
NT = 8
TT = 512
NQ = 4
DC = 16
FC = 16
EPS = 1e-5
GOLDEN_CENTER = 1.0 / math.e
GOLDEN_LOWER = 0.5 - math.log(4.0 / 3.0)
GOLDEN_UPPER = 0.5
SCALING = DH ** (-0.5)

ENGINES = ("tensor", "vector", "scalar", "sync", "gpsimd")
ESEM = {"tensor": "pe", "vector": "dve", "scalar": "act"}
DMA_SEMS = (["q_cn", "q_stg", "q_wkv", "q_ctw", "q_swq", "q_xy", "q_ctm", "q_sxb",
             "q_sy", "q_out", "q_dbg"]
            + [f"q_wq{i}" for i in range(2)] + [f"q_wo{i}" for i in range(4)]
            + [f"q_gu{i}" for i in range(4)] + [f"q_wd{i}" for i in range(2)]
            + [f"q_xr{i}" for i in range(2)] + [f"q_yb{i}" for i in range(2)]
            + [f"q_ct{i}" for i in range(8)])
ALL_SEMS = ["pe", "dve", "act"] + DMA_SEMS


class Res:
    __slots__ = ("w", "rs")

    def __init__(self):
        self.w = None
        self.rs = []

    def for_write(self):
        d = list(self.rs)
        if self.w is not None:
            d.append(self.w)
        return d

    def for_read(self):
        return [self.w] if self.w is not None else []

    def all_evs(self):
        return self.for_write()

    def set_w(self, ev):
        self.w = ev
        self.rs = []

    def add_r(self, ev):
        self.rs.append(ev)


def seed(new_reses, old_reses):
    evs = []
    for o in old_reses:
        evs += o.all_evs()
    for n in new_reses:
        n.rs.extend(evs)


class KB:
    def __init__(self, sems):
        self.sems = sems
        self.ops = {e: [] for e in ENGINES}
        self.cnt = {s: 0 for s in ALL_SEMS}
        self.lastw = {}

    def emit(self, eng, fn, deps=(), inc=None, inc_n=1):
        seen = {}
        for d in deps:
            if d is None:
                continue
            s, v = d
            if v <= 0:
                continue
            if s.startswith("q_"):
                v = self.cnt[s]
                if v <= 0:
                    continue
            if seen.get(s, 0) < v:
                seen[s] = v
        waits = []
        for s, v in seen.items():
            if eng == "tensor" and s == "pe":
                continue
            if self.lastw.get((eng, s), 0) >= v:
                continue
            self.lastw[(eng, s)] = v
            waits.append((s, v))
        ev = None
        if inc is not None:
            self.cnt[inc] += inc_n
            ev = (inc, self.cnt[inc])
        self.ops[eng].append((fn, waits, inc, inc_n))
        return ev

    def dma(self, q, out, in_, reads=(), wres=None, tp=False, sem="q_cn"):
        deps = list(wres.for_write()) if wres is not None else []
        for r in reads:
            deps += r.for_read()
        if tp:
            fn = lambda e, o=out, i=in_: e.dma_start_transpose(out=o, in_=i)
        else:
            fn = lambda e, o=out, i=in_: e.dma_start(out=o, in_=i)
        ev = self.emit(q, fn, deps, inc=sem, inc_n=16)
        if wres is not None:
            wres.set_w(ev)
        for r in reads:
            r.add_r(ev)
        return ev

    def mmgroup(self, mms, bank, reads=()):
        deps = list(bank.for_write())
        for r in reads:
            deps += r.for_read()
        n = len(mms)
        ev = None
        for i, (o, l, r_) in enumerate(mms):
            st, sp = (i == 0), (i == n - 1)
            fn = (lambda e, o=o, l=l, r_=r_, st=st, sp=sp:
                  e.matmul(o, l, r_, start=st, stop=sp))
            ev = self.emit("tensor", fn, deps if i == 0 else (),
                           inc="pe" if sp else None)
        bank.set_w(ev)
        for r in reads:
            r.add_r(ev)
        return ev

    def tmm(self, out, in_, ident, bank, reads=(), ident_res=None):
        deps = list(bank.for_write())
        for r in reads:
            deps += r.for_read()
        if ident_res is not None:
            deps += ident_res.for_read()
        fn = lambda e, o=out, i=in_, d=ident: e.matmul(o, i, d, start=True,
                                                       stop=True)
        ev = self.emit("tensor", fn, deps, inc="pe")
        bank.set_w(ev)
        for r in reads:
            r.add_r(ev)
        if ident_res is not None:
            ident_res.add_r(ev)
        return ev

    def transpose(self, out, in_, ident, bank, reads=(), ident_res=None):
        deps = list(bank.for_write())
        for r in reads:
            deps += r.for_read()
        if ident_res is not None:
            deps += ident_res.for_read()
        fn = lambda e, o=out, i=in_, d=ident: e.transpose(o, i, d)
        ev = self.emit("tensor", fn, deps, inc="pe")
        bank.set_w(ev)
        for r in reads:
            r.add_r(ev)
        if ident_res is not None:
            ident_res.add_r(ev)
        return ev

    def op(self, eng, fn, reads=(), wres=None, wres2=None):
        wl = []
        if wres is not None:
            wl = list(wres) if isinstance(wres, list) else [wres]
        if wres2 is not None:
            wl.append(wres2)
        deps = []
        for w in wl:
            deps += w.for_write()
        for r in reads:
            deps += r.for_read()
        ev = self.emit(eng, fn, deps, inc=ESEM[eng])
        for w in wl:
            w.set_w(ev)
        for r in reads:
            r.add_r(ev)
        return ev


def ts(i, n):
    return slice(i * n, (i + 1) * n)


def build_nc(num_tiles=NT, dbg=False):
    nc = bass.Bass("TRN2", target_bir_lowering=False)
    skind = {"kind": "ExternalOutput"} if dbg else {}

    x_in = nc.dram_tensor("x", [T, D], F32, kind="ExternalInput").ap()
    wqT = nc.dram_tensor("wqT", [D, D], F32, kind="ExternalInput").ap()
    wkv_pm = nc.dram_tensor("wkv_pm", [2, 128, DC * 128], F32, kind="ExternalInput").ap()
    wo_blk = nc.dram_tensor("wo_blk", [8, 128, DC * 256], F32, kind="ExternalInput").ap()
    wg_blk = nc.dram_tensor("wg_blk", [E, 8, 128, DC * 256], F32, kind="ExternalInput").ap()
    wu_blk = nc.dram_tensor("wu_blk", [E, 8, 128, DC * 256], F32, kind="ExternalInput").ap()
    wd_blk = nc.dram_tensor("wd_blk", [E, 8, 128, FC * 256], F32, kind="ExternalInput").ap()
    ga_in = nc.dram_tensor("ga", [NL, D, R], F32, kind="ExternalInput").ap()
    gb_in = nc.dram_tensor("gb", [NL, R, F], F32, kind="ExternalInput").ap()
    ua_in = nc.dram_tensor("ua", [NL, D, R], F32, kind="ExternalInput").ap()
    ub_in = nc.dram_tensor("ub", [NL, R, F], F32, kind="ExternalInput").ap()
    da_in = nc.dram_tensor("da", [NL, F, R], F32, kind="ExternalInput").ap()
    db_in = nc.dram_tensor("db", [NL, R, D], F32, kind="ExternalInput").ap()
    wr_in = nc.dram_tensor("wr", [D, E], F32, kind="ExternalInput").ap()
    temp_in = nc.dram_tensor("temp", [1, 1], F32, kind="ExternalInput").ap()
    cc_identb = nc.dram_tensor("cc_identb", [128, 128], BF16, kind="ExternalInput").ap()
    cc_identf = nc.dram_tensor("cc_identf", [128, 128], F32, kind="ExternalInput").ap()
    cc_mask = nc.dram_tensor("cc_mask", [128, 128], F32, kind="ExternalInput").ap()
    cc_ropec = nc.dram_tensor("cc_ropec", [128, D], F32, kind="ExternalInput").ap()
    cc_ropes = nc.dram_tensor("cc_ropes", [128, D], F32, kind="ExternalInput").ap()
    cc_onesb = nc.dram_tensor("cc_onesb", [1, 128], BF16, kind="ExternalInput").ap()
    cc_onesf = nc.dram_tensor("cc_onesf", [1, 128], F32, kind="ExternalInput").ap()

    out_ap = nc.dram_tensor("out", [T, D], F32, kind="ExternalOutput").ap()
    if dbg:
        dbg_k = nc.dram_tensor("dbg_k", [128, T], BF16, kind="ExternalOutput").ap()
        dbg_v = nc.dram_tensor("dbg_v", [128, T], BF16, kind="ExternalOutput").ap()
        dbg_q = nc.dram_tensor("dbg_q", [128, TT], BF16, kind="ExternalOutput").ap()
        dbg_S = nc.dram_tensor("dbg_S", [128, 4096], F32, kind="ExternalOutput").ap()
        dbg_P = nc.dram_tensor("dbg_P", [128, 4096], BF16, kind="ExternalOutput").ap()
        dbg_att = nc.dram_tensor("dbg_att", [H, 128, TT], BF16, kind="ExternalOutput").ap()
        dbg_res = nc.dram_tensor("dbg_res", [128, D], F32, kind="ExternalOutput").ap()
        dbg_xfm = nc.dram_tensor("dbg_xfm", [128, DC * TT], BF16, kind="ExternalOutput").ap()
        dbg_rsum = nc.dram_tensor("dbg_rsum", [128, NQ], F32, kind="ExternalOutput").ap()
        dbg_bc = nc.dram_tensor("dbg_bc", [128, 256], F32, kind="ExternalOutput").ap()
        dbg_pt = nc.dram_tensor("dbg_pt", [4, 128, 256], BF16, kind="ExternalOutput").ap()

    s_xbf = nc.dram_tensor("s_xbf", [T, D], BF16, **skind).ap()
    s_y = nc.dram_tensor("s_y", [T, D], BF16, **skind).ap()
    s_wq = nc.dram_tensor("s_wq", [H, 128, DC * 128], BF16, **skind).ap()
    s_wo = nc.dram_tensor("s_wo", [8, 128, DC * 256], BF16).ap()
    s_wg = nc.dram_tensor("s_wg", [E, 8, 128, DC * 256], BF16).ap()
    s_wu = nc.dram_tensor("s_wu", [E, 8, 128, DC * 256], BF16).ap()
    s_wd = nc.dram_tensor("s_wd", [E, 8, 128, FC * 256], BF16).ap()
    s_wkv = nc.dram_tensor("s_wkv", [2, 128, DC * 128], BF16).ap()

    es = ExitStack()
    sb = lambda name, shape, dt: es.enter_context(nc.sbuf_tensor(name, shape, dt))
    psum = lambda name, shape, dt: es.enter_context(nc.psum_tensor(name, shape, dt))

    identb = sb("identb", [128, 128], BF16)
    identf = sb("identf", [128, 128], F32)
    mask_sb = sb("mask_sb", [128, 128], F32)
    onesb = sb("onesb", [1, 128], BF16)
    onesf = sb("onesf", [1, 128], F32)
    wr_sb = sb("wr_sb", [128, DC * E], BF16)
    nrt_sb = sb("nrt_sb", [128, 1], F32)
    t_tmp = sb("t_tmp", [1, 2], F32)
    ga_sb = sb("ga_sb", [128, NL * DC * R], BF16)
    ua_sb = sb("ua_sb", [128, NL * DC * R], BF16)
    da_sb = sb("da_sb", [128, NL * FC * R], BF16)
    gb_sb = sb("gb_sb", [128, F], BF16)
    ub_sb = sb("ub_sb", [128, F], BF16)
    db_sb = sb("db_sb", [128, D], BF16)
    k_fm = sb("k_fm", [128, T], BF16)
    v_tm = sb("v_tm", [128, T], BF16)
    wkv_sb = sb("wkv_sb", [128, 2 * DC * 128], BF16)
    xy_fm = sb("xy_fm", [128, DC * TT], BF16)
    arena_bf = sb("arena_bf", [128, 32768], BF16)
    arena_f = sb("arena_f", [128, 12288], F32)
    x_ring = sb("x_ring", [128, 2 * 512], F32)
    ybf_ring = sb("ybf_ring", [128, 2 * D], BF16)
    sg_ring = sb("sg_ring", [128, 2 * 512], BF16)
    t1_ring = sb("t1_ring", [128, 2 * 512], F32)
    wb_sb = sb("wb_sb", [128, 512], F32)
    bcast_sb = sb("bcast_sb", [128, 256], F32)
    a_fm = sb("a_fm", [128, TT], BF16)
    au_fm = sb("au_fm", [128, TT], BF16)
    am_fm = sb("am_fm", [128, TT], BF16)
    aev_sb = sb("aev_sb", [128, 64], BF16)
    srow_sb = sb("srow_sb", [1, 256], F32)
    rrow_sb = sb("rrow_sb", [1, 512], BF16)
    negmax_sb = sb("negmax_sb", [128, NQ], F32)
    rsum_sb = sb("rsum_sb", [128, NQ], F32)
    sm2_sb = sb("sm2_sb", [128, 2], F32)
    smax_sb = sb("smax_sb", [128, NQ * 8], F32)
    bnt_sb = sb("bnt_sb", [128, 24], F32)
    mv_sb = sb("mv_sb", [128, 2], F32)
    ln_sc = sb("ln_sc", [128, 4], F32)
    rt_sb = sb("rt_sb", [128, 12 * E], F32)
    rt1_sb = sb("rt1_sb", [128, 8], F32)
    w_tm = sb("w_tm", [128, NQ * E], F32)

    ps_mmA = psum("ps_mmA", [128, 512], F32)
    ps_mmB = psum("ps_mmB", [128, 512], F32)
    ps_mmC = psum("ps_mmC", [128, 512], F32)
    ps_pv = psum("ps_pv", [128, 512], F32)
    ps_tb1 = psum("ps_tb1", [128, 512], F32)
    ps_tb2 = psum("ps_tb2", [128, 512], F32)
    ps_tf = psum("ps_tf", [128, 512], F32)
    ps_bc = psum("ps_bc", [128, 512], F32)

    sems = {}
    for s in ALL_SEMS:
        sems[s] = es.enter_context(nc.semaphore("sem_" + s))
    kb = KB(sems)

    R_ = Res
    r_identb, r_identf, r_mask, r_onesb, r_onesf = R_(), R_(), R_(), R_(), R_()
    r_wr, r_nrt, r_ttmp = R_(), R_(), R_()
    r_ga, r_ua, r_da, r_gb, r_ub, r_db = R_(), R_(), R_(), R_(), R_(), R_()
    r_kfm, r_vtm, r_wkv = R_(), R_(), R_()
    r_xyfm = R_()
    r_xring = [R_() for _ in range(2)]
    r_ybf = [R_() for _ in range(2)]
    r_sg = [R_() for _ in range(2)]
    r_t1 = [R_() for _ in range(2)]
    r_wb, r_bcast = R_(), R_()
    r_afm, r_aufm, r_amfm, r_aev = R_(), R_(), R_(), R_()
    r_srow, r_rrow = R_(), R_()
    r_negmax, r_rsum, r_sm2, r_smax = R_(), R_(), R_(), R_()
    r_bnt, r_mv, r_lnsc = R_(), R_(), R_()
    r_rt, r_rt1, r_wtm = R_(), R_(), R_()
    r_psA, r_psB, r_psC, r_psPV = R_(), R_(), R_(), R_()
    r_psT1, r_psT2, r_psTF, r_psBC = R_(), R_(), R_(), R_()
    r_sxbf = [R_() for _ in range(32)]
    r_sy = [R_() for _ in range(32)]
    r_swq, r_swo, r_swkv = R_(), R_(), R_()
    r_swg = [R_() for _ in range(E)]
    r_swu = [R_() for _ in range(E)]
    r_swd = [R_() for _ in range(E)]
    r_out = R_()

    A_PT = lambda kc: arena_bf[:, kc * 256:(kc + 1) * 256]
    A_PSB = arena_bf[:, 8192:12288]
    A_WQ = lambda j: arena_bf[:, 12288 + j * 2048:12288 + (j + 1) * 2048]
    A_QBF = arena_bf[:, 16384:16896]
    A_ATT = lambda h: arena_bf[:, 17408 + h * 512:17408 + (h + 1) * 512]
    A_WO = lambda j: arena_bf[:, j * 4096:(j + 1) * 4096]
    A_WGU = lambda j: arena_bf[:, j * 4096:(j + 1) * 4096]
    A_WD = lambda j: arena_bf[:, 16384 + j * 4096:16384 + (j + 1) * 4096]
    A_MID = lambda fc: arena_bf[:, 24576 + fc * 512:24576 + (fc + 1) * 512]
    r_PT = [R_() for _ in range(32)]
    r_psb, r_qbf = R_(), R_()
    r_wq = [R_() for _ in range(2)]
    r_att = [R_() for _ in range(H)]
    r_wo = [R_() for _ in range(4)]
    r_wgu = [R_() for _ in range(4)]
    r_wd = [R_() for _ in range(2)]
    r_mid = [R_() for _ in range(FC)]

    A_ACC = lambda dc: arena_f[:, dc * 512:(dc + 1) * 512]
    A_RES4 = lambda qs: arena_f[:, qs * 2048:(qs + 1) * 2048]
    A_S = arena_f[:, 8192:12288]
    A_SROW = lambda r: arena_f[:, r * 4096:(r + 1) * 4096]
    A_PROW = (arena_bf[:, 8192:12288], arena_bf[:, 25600:29696])
    A_RES = lambda j: arena_f[:, 8192 + j * 2048:8192 + (j + 1) * 2048]
    P_CC = arena_f[:, 0:2048]
    P_SS = arena_f[:, 2048:4096]
    P_STG = arena_f[:, 4096:6144]
    P_ROT = arena_f[:, 6144:8192]
    r_acc = [R_() for _ in range(DC)]
    r_res4 = [R_() for _ in range(NQ)]
    r_S = R_()
    r_Sb = [R_() for _ in range(8)]
    r_Srow = [[R_() for _ in range(8)] for _ in range(3)]
    r_Prow = [R_(), R_()]
    r_res = [R_() for _ in range(2)]
    r_pcc, r_pss, r_pstg, r_prot, r_ptmp = R_(), R_(), R_(), R_(), R_()
    r_wqstg = R_()

    with es:
        kb.dma("sync", identb[:], cc_identb, wres=r_identb)
        kb.dma("sync", identf[:], cc_identf, wres=r_identf)
        kb.dma("sync", mask_sb[:], cc_mask, wres=r_mask)
        kb.dma("sync", onesb[:], cc_onesb, wres=r_onesb)
        kb.dma("sync", onesf[:], cc_onesf, wres=r_onesf)
        kb.dma("sync", P_CC, cc_ropec, wres=r_pcc)
        kb.dma("sync", P_SS, cc_ropes, wres=r_pss)
        kb.dma("sync", t_tmp[0:1, 0:1], temp_in, wres=r_ttmp)
        kb.op("vector", lambda e: e.reciprocal(out=t_tmp[0:1, 1:2], in_=t_tmp[0:1, 0:1]),
              reads=[r_ttmp], wres=r_ttmp)
        kb.op("vector", lambda e: e.tensor_scalar(out=t_tmp[0:1, 1:2],
                                                  in0=t_tmp[0:1, 1:2],
                                                  scalar1=-1.0, scalar2=None,
                                                  op0=ALU.mult),
              reads=[r_ttmp], wres=r_ttmp)
        kb.mmgroup([(ps_bc[:, 0:1], onesf[:], t_tmp[0:1, 1:2])],
                   r_psBC, reads=[r_onesf, r_ttmp])
        kb.op("scalar", lambda e: e.copy(out=nrt_sb[:], in_=ps_bc[:, 0:1]),
              reads=[r_psBC], wres=r_nrt)

        kb.dma("gpsimd", s_wkv.rearrange("a p d -> (a p) d"),
               wkv_pm.rearrange("a p d -> (a p) d"), wres=r_swkv, sem="q_ctw")
        kb.dma("sync", wkv_sb[:, 0:DC * 128], s_wkv[0], reads=[r_swkv],
               wres=r_wkv, sem="q_wkv")
        kb.dma("sync", wkv_sb[:, DC * 128:2 * DC * 128], s_wkv[1],
               reads=[r_swkv], wres=r_wkv, sem="q_wkv")
        for st in range(4 * num_tiles):
            sl = st % 2
            for half in range(2):
                xsl = half
                kb.dma("scalar", x_ring[:, ts(xsl, 512)],
                       x_in[ts(st, 128), ts(2 * half, 512)],
                       wres=r_xring[xsl], sem=f"q_xr{xsl}")
                kb.op("scalar", lambda e, sl=sl, half=half, xsl=xsl: e.copy(
                    out=ybf_ring[:, sl * D + 1024 * half: sl * D + 1024 * half + 512],
                    in_=x_ring[:, ts(xsl, 512)]),
                    reads=[r_xring[xsl]], wres=r_ybf[sl])
                kb.dma("scalar", x_ring[:, ts(xsl, 512)],
                       x_in[ts(st, 128), ts(2 * half + 1, 512)],
                       wres=r_xring[xsl], sem=f"q_xr{xsl}")
                kb.op("scalar", lambda e, sl=sl, half=half, xsl=xsl: e.copy(
                    out=ybf_ring[:, sl * D + 1024 * half + 512:
                                sl * D + 1024 * half + 1024],
                    in_=x_ring[:, ts(xsl, 512)]),
                    reads=[r_xring[xsl]], wres=r_ybf[sl])
            kb.dma("scalar", s_xbf[ts(st, 128), :], ybf_ring[:, ts(sl, D)],
                   reads=[r_ybf[sl]], wres=r_sxbf[st], sem="q_sxb")
            if st == 3:
                for dc in range(DC):
                    kb.dma("scalar" if dc % 2 == 0 else "sync",
                           xy_fm[:, ts(dc, TT)], s_xbf[0:TT, ts(dc, 128)],
                           reads=[r_sxbf[j] for j in range(4)],
                           wres=r_xyfm, tp=True, sem="q_xy")

        AROW = 12288
        ap_tmp = bass.AP(arena_f, 8192, [[AROW, 128], [1, 2048]])
        ap_te = bass.AP(arena_f, 8192, [[AROW, 128], [2, 1024]])
        ap_to = bass.AP(arena_f, 8193, [[AROW, 128], [2, 1024]])
        ap_se = bass.AP(arena_f, 4096, [[AROW, 128], [2, 1024]])
        ap_so = bass.AP(arena_f, 4097, [[AROW, 128], [2, 1024]])
        ap_ce = bass.AP(arena_f, 2048, [[AROW, 128], [2, 1024]])
        ap_co = bass.AP(arena_f, 2049, [[AROW, 128], [2, 1024]])
        for dc in range(DC):
            kb.dma("sync", P_STG, wqT[ts(dc, 128), :], wres=r_pstg,
                   sem="q_stg")
            kb.op("vector", lambda e: e.tensor_tensor(out=P_ROT, in0=P_STG,
                                                      in1=P_CC, op=ALU.mult),
                  reads=[r_pstg, r_pcc], wres=r_prot)
            kb.op("vector", lambda e: e.tensor_tensor(
                out=ap_te, in0=ap_so, in1=ap_ce, op=ALU.mult),
                reads=[r_pstg, r_pss], wres=r_ptmp)
            kb.op("vector", lambda e: e.tensor_tensor(
                out=ap_to, in0=ap_se, in1=ap_co, op=ALU.mult),
                reads=[r_pstg, r_pss], wres=r_ptmp)
            kb.op("vector", lambda e: e.tensor_tensor(
                out=arena_bf[:, 0:2048], in0=P_ROT, in1=ap_tmp, op=ALU.add),
                reads=[r_prot, r_ptmp], wres=r_wqstg)
            for half in range(2):
                    dst = bass.AP(s_wq.tensor,
                                  (half * 8) * (128 * DC * 128) + dc * 128,
                                  [[DC * 128, 128], [128 * DC * 128, 8], [1, 128]])
                    srcv = bass.AP(arena_bf, half * 8 * 128,
                                   [[32768, 128], [128, 8], [1, 128]])
                    kb.dma("sync" if half == 0 else "scalar", dst, srcv,
                           reads=[r_wqstg], wres=r_swq, sem="q_swq")

        for l in range(NL):
            for dc in range(DC):
                kb.dma("gpsimd", ga_sb[:, ts(l * DC + dc, R)],
                       ga_in[l, ts(dc, 128), :], wres=r_ga, sem="q_ctm")
                kb.dma("gpsimd", ua_sb[:, ts(l * DC + dc, R)],
                       ua_in[l, ts(dc, 128), :], wres=r_ua, sem="q_ctm")
                kb.dma("gpsimd", da_sb[:, ts(l * DC + dc, R)],
                       da_in[l, ts(dc, 128), :], wres=r_da, sem="q_ctm")
            kb.dma("gpsimd", gb_sb[64 * l:64 * l + 64, :], gb_in[l], wres=r_gb, sem="q_ctm")
            kb.dma("gpsimd", ub_sb[64 * l:64 * l + 64, :], ub_in[l], wres=r_ub, sem="q_ctm")
            kb.dma("gpsimd", db_sb[64 * l:64 * l + 64, :], db_in[l], wres=r_db, sem="q_ctm")
        for dc in range(DC):
            kb.dma("gpsimd", wr_sb[:, ts(dc, E)], wr_in[ts(dc, 128), :], wres=r_wr, sem="q_ctm")
        kb.dma("gpsimd", s_wo.rearrange("a p d -> (a p) d"),
               wo_blk.rearrange("a p d -> (a p) d"), wres=r_swo, sem="q_ctm")

        boundary_res = [r_pcc, r_pss, r_pstg, r_prot, r_ptmp, r_wqstg,
                        r_ybf[0], r_ybf[1], r_xring[0], r_xring[1], r_t1[0]]

        for t in range(num_tiles):
            tile_rows = ts(t, TT)

            if t == 0:
                seed(r_PT + [r_psb, r_qbf, r_S] + r_Sb + r_Srow[0]
                     + r_Srow[1] + r_Srow[2] + r_Prow + r_wq + r_att + r_acc
                     + r_res, boundary_res)
            else:
                seed(r_PT + [r_psb, r_qbf] + r_wq + r_att, r_wgu + r_wd + r_mid)
                seed([r_S] + r_Sb + r_Srow[0] + r_Srow[1] + r_Srow[2],
                     r_res + r_acc + r_res4)
                seed(r_Prow, r_mid)

            if t > 0:
                for dc in range(DC):
                    kb.dma("scalar" if dc % 2 == 0 else "sync",
                           xy_fm[:, ts(dc, TT)],
                           s_xbf[tile_rows, ts(dc, 128)],
                           reads=[r_sxbf[4 * t + j] for j in range(4)],
                           wres=r_xyfm, tp=True, sem="q_xy")

            kb.mmgroup([(ps_mmA[:], wkv_sb[:, ts(dc, 128)], xy_fm[:, ts(dc, TT)])
                        for dc in range(DC)], r_psA, reads=[r_wkv, r_xyfm])
            kb.op("scalar", lambda e, t=t: e.copy(out=k_fm[:, ts(t, TT)],
                                                  in_=ps_mmA[:]),
                  reads=[r_psA], wres=r_kfm)
            for qs in range(NQ):
                kb.mmgroup([(ps_mmB[:, 0:128],
                             xy_fm[:, dc * TT + qs * 128:dc * TT + (qs + 1) * 128],
                             wkv_sb[:, ts(DC + dc, 128)]) for dc in range(DC)],
                           r_psB, reads=[r_wkv, r_xyfm])
                kb.op("scalar", lambda e, t=t, qs=qs: e.copy(
                    out=v_tm[:, ts(4 * t + qs, 128)], in_=ps_mmB[:, 0:128]),
                    reads=[r_psB], wres=r_vtm)

            if dbg and t == 0:
                kb.dma("gpsimd", dbg_k, k_fm[:], reads=[r_kfm], sem="q_dbg")
                kb.dma("gpsimd", dbg_v, v_tm[:], reads=[r_vtm], sem="q_dbg")
                kb.dma("gpsimd", dbg_xfm, xy_fm[:], reads=[r_xyfm], sem="q_dbg")

            for h in range(H):
                if t == 0 and h % 2 == 0:
                    e_ = h // 2
                    gate = [("pe", kb.cnt["pe"])]
                    for smat, dmat, rres in ((s_wg, wg_blk, r_swg[e_]),
                                             (s_wu, wu_blk, r_swu[e_]),
                                             (s_wd, wd_blk, r_swd[e_])):
                        deps = list(gate)
                        fn = (lambda e, o=smat[e_].rearrange("a p d -> (a p) d"),
                              i=dmat[e_].rearrange("a p d -> (a p) d"):
                              e.dma_start(out=o, in_=i))
                        ev = kb.emit("gpsimd", fn, deps, inc=f"q_ct{e_}",
                                     inc_n=16)
                        rres.set_w(ev)
                wq_slot = h % 2
                kb.dma("sync", A_WQ(wq_slot), s_wq[h], reads=[r_swq],
                       wres=r_wq[wq_slot], sem=f"q_wq{wq_slot}")
                kb.mmgroup([(ps_mmA[:], A_WQ(wq_slot)[:, ts(dc, 128)],
                             xy_fm[:, ts(dc, TT)]) for dc in range(DC)],
                           r_psA, reads=[r_wq[wq_slot], r_xyfm])
                kb.op("vector", lambda e: e.tensor_copy(out=A_QBF,
                                                             in_=ps_mmA[:]),
                      reads=[r_psA], wres=r_qbf)
                def emit_S(qs):
                    qi = 4 * t + qs
                    W = (qi + 1) * 128
                    row = qs % 3
                    for kb_i in range(t + 1):
                        w0 = kb_i * 512
                        wN = min(512, W - w0)
                        bank, rbank = [(ps_mmB, r_psB), (ps_mmC, r_psC),
                                       (ps_mmA, r_psA)][kb_i % 3]
                        kb.mmgroup([(bank[:, 0:wN], A_QBF[:, ts(qs, 128)],
                                     k_fm[:, w0:w0 + wN])],
                                   rbank, reads=[r_qbf, r_kfm])
                        rs_ = r_Srow[row][kb_i]
                        if kb_i < t:
                            kb.op("vector", lambda e, w0=w0, wN=wN, bank=bank,
                                  row=row, qs=qs, kb_i=kb_i: e.tensor_scalar(
                                out=A_SROW(row)[:, w0:w0 + wN],
                                in0=bank[:, 0:wN], scalar1=0.0, scalar2=None,
                                op0=ALU.add, op1=ALU.max,
                                accum_out=smax_sb[:, qs * 8 + kb_i:
                                                  qs * 8 + kb_i + 1]),
                                reads=[rbank], wres=rs_, wres2=r_smax)
                        else:
                            if wN > 128:
                                kb.op("vector", lambda e, w0=w0, wN=wN, bank=bank,
                                      row=row: e.tensor_copy(
                                    out=A_SROW(row)[:, w0:w0 + wN - 128],
                                    in_=bank[:, 0:wN - 128]),
                                    reads=[rbank], wres=rs_)
                            kb.op("vector", lambda e, w0=w0, wN=wN, bank=bank,
                                  row=row: e.tensor_tensor(
                                out=A_SROW(row)[:, w0 + wN - 128:w0 + wN],
                                in0=bank[:, wN - 128:wN], in1=mask_sb[:],
                                op=ALU.add), reads=[rbank, r_mask], wres=rs_)
                            kb.op("vector", lambda e, w0=w0, wN=wN, row=row,
                                  qs=qs, kb_i=kb_i: e.reduce_max(
                                out=smax_sb[:, qs * 8 + kb_i:qs * 8 + kb_i + 1],
                                in_=A_SROW(row)[:, w0:w0 + wN], axis=AX.X),
                                reads=[rs_], wres=r_smax)
                    kb.op("vector", lambda e, qs=qs, nb=t + 1: e.reduce_max(
                        out=negmax_sb[:, qs:qs + 1],
                        in_=smax_sb[:, qs * 8:qs * 8 + nb], axis=AX.X,
                        negate=True), reads=[r_smax], wres=r_negmax)

                def emit_T(qs):
                    qi = 4 * t + qs
                    W = (qi + 1) * 128
                    row = qs % 3
                    prow = qs % 2
                    hs = (qs % 2) * 128
                    nblk = t + 1
                    for bb in range((nblk + 1) // 2):
                        w0 = bb * 1024
                        wN = min(1024, W - w0)
                        rds = [r_Srow[row][2 * bb], r_negmax]
                        if 2 * bb + 1 < nblk:
                            rds.append(r_Srow[row][2 * bb + 1])
                        kb.op("scalar", lambda e, w0=w0, wN=wN, qs=qs,
                              row=row, prow=prow:
                              e.activation(out=A_PROW[prow][:, w0:w0 + wN],
                                           in_=A_SROW(row)[:, w0:w0 + wN],
                                           func=AF.Exp,
                                           bias=negmax_sb[:, qs:qs + 1],
                                           scale=1.0, accum_out=sm2_sb[:, 1:2]),
                              reads=rds, wres=r_Prow[prow], wres2=r_sm2)
                        if bb == 0:
                            kb.op("vector", lambda e, qs=qs: e.tensor_copy(
                                out=rsum_sb[:, qs:qs + 1], in_=sm2_sb[:, 1:2]),
                                reads=[r_sm2], wres=r_rsum)
                        else:
                            kb.op("vector", lambda e, qs=qs: e.tensor_tensor(
                                out=rsum_sb[:, qs:qs + 1],
                                in0=rsum_sb[:, qs:qs + 1],
                                in1=sm2_sb[:, 1:2], op=ALU.add),
                                reads=[r_sm2, r_rsum], wres=r_rsum)
                        for half8 in range(2):
                            kc0 = bb * 8 + half8 * 4
                            if kc0 > qi:
                                break
                            nk4 = min(qi + 1, kc0 + 4) - kc0
                            tb, rtb = ((ps_tb1, r_psT1) if half8 == 0
                                       else (ps_tb2, r_psT2))
                            for j in range(nk4):
                                kb.tmm(tb[:, ts(j, 128)],
                                       A_PROW[prow][:, ts(kc0 + j, 128)],
                                       identb[:], rtb, reads=[r_Prow[prow]],
                                       ident_res=r_identb)
                            pt_dst = bass.AP(arena_bf, kc0 * 256 + hs,
                                             [[32768, 128], [256, nk4], [1, 128]])
                            wl = [r_PT[kc0 + j] for j in range(nk4)]
                            kb.op("vector", lambda e, d=pt_dst, tb=tb, nk4=nk4:
                                  e.tensor_copy(out=d, in_=tb[:, 0:nk4 * 128]),
                                  reads=[rtb], wres=wl)
                    if qs % 2 == 1:
                        kb.op("vector", lambda e, qi=qi: e.memset(
                            A_PT(qi)[:, 0:128], 0.0), wres=r_PT[qi])

                def emit_PV(Hh):
                    qs_hi = 2 * Hh + 1
                    nk = 4 * t + qs_hi + 1
                    kb.mmgroup([(ps_pv[:, 0:256], v_tm[:, ts(kc, 128)],
                                 A_PT(kc)[:, 0:256]) for kc in range(nk)],
                               r_psPV,
                               reads=[r_vtm] + [r_PT[kc] for kc in range(nk)])

                def emit_norm_a(Hh):
                    for j in range(2):
                        qsj = 2 * Hh + j
                        kb.transpose(ps_tf[0:1, ts(j, 128)],
                                     rsum_sb[:, qsj:qsj + 1], identf[:],
                                     r_psTF, reads=[r_rsum], ident_res=r_identf)
                    kb.op("scalar", lambda e: e.copy(out=srow_sb[0:1, 0:256],
                                                     in_=ps_tf[0:1, 0:256]),
                          reads=[r_psTF], wres=r_srow)
                    kb.op("vector", lambda e: e.reciprocal(
                        out=rrow_sb[0:1, 0:256], in_=srow_sb[0:1, 0:256]),
                        reads=[r_srow], wres=r_rrow)

                def emit_norm_b(Hh, h):
                    kb.mmgroup([(ps_bc[:, 0:256], onesb[:], rrow_sb[0:1, 0:256])],
                               r_psBC, reads=[r_onesb, r_rrow])
                    kb.op("scalar", lambda e: e.copy(out=bcast_sb[:],
                                                     in_=ps_bc[:, 0:256]),
                          reads=[r_psBC], wres=r_bcast)
                    kb.op("vector", lambda e, h=h, Hh=Hh: e.tensor_tensor(
                        out=A_ATT(h)[:, ts(Hh, 256)], in0=ps_pv[:, 0:256],
                        in1=bcast_sb[:], op=ALU.mult),
                        reads=[r_psPV, r_bcast], wres=r_att[h])

                emit_S(0)
                emit_S(1)
                emit_S(2)
                emit_T(0)
                emit_S(3)
                emit_T(1)
                emit_PV(0)
                emit_norm_a(0)
                emit_T(2)
                emit_norm_b(0, h)
                emit_T(3)
                emit_norm_a(1)
                emit_PV(1)
                emit_norm_b(1, h)

            if dbg and t == 0:
                for h in range(H):
                    kb.dma("gpsimd", dbg_att[h], A_ATT(h), reads=[r_att[h]], sem="q_dbg")

            seed(r_wo, r_PT + [r_psb] + r_wq + [r_qbf])
            seed(r_res4, r_acc + r_Srow[0] + r_Srow[1])
            seed(r_res, r_Srow[2])
            seed(r_res, [r_S])

            def ln_pass(res_ap, res_res, out_dst, out_wres):
                kb.op("vector", lambda e, res_ap=res_ap: e.bn_stats(
                    out=bnt_sb[:, 0:6], in_=res_ap[:, 0:512]),
                    reads=[res_res], wres=r_bnt)
                for kq in range(1, 4):
                    kb.op("vector", lambda e, kq=kq, res_ap=res_ap: e.bn_stats(
                        out=bnt_sb[:, ts(kq, 6)], in_=res_ap[:, ts(kq, 512)]),
                        reads=[res_res], wres=r_bnt)
                kb.op("vector", lambda e: e.bn_aggr(out=mv_sb[:], in_=bnt_sb[:]),
                      reads=[r_bnt], wres=r_mv)
                kb.op("vector", lambda e: e.tensor_scalar(
                    out=ln_sc[:, 0:1], in0=mv_sb[:, 1:2], scalar1=EPS, scalar2=None,
                    op0=ALU.add), reads=[r_mv], wres=r_lnsc)
                kb.op("scalar", lambda e: e.activation(
                    out=ln_sc[:, 3:4], in_=ln_sc[:, 0:1], func=AF.Ln),
                    reads=[r_lnsc], wres=r_lnsc)
                kb.op("vector", lambda e: e.tensor_scalar(
                    out=ln_sc[:, 3:4], in0=ln_sc[:, 3:4], scalar1=-0.5, scalar2=None,
                    op0=ALU.mult), reads=[r_lnsc], wres=r_lnsc)
                kb.op("scalar", lambda e: e.activation(
                    out=ln_sc[:, 1:2], in_=ln_sc[:, 3:4], func=AF.Exp),
                    reads=[r_lnsc], wres=r_lnsc)
                kb.op("vector", lambda e: e.scalar_tensor_tensor(
                    out=ln_sc[:, 2:3], in0=mv_sb[:, 0:1], scalar=-1.0,
                    in1=ln_sc[:, 1:2], op0=ALU.mult, op1=ALU.mult),
                    reads=[r_mv, r_lnsc], wres=r_lnsc)
                kb.op("scalar", lambda e, res_ap=res_ap, out_dst=out_dst:
                      e.activation(out=out_dst, in_=res_ap[:],
                                   func=AF.Identity, scale=ln_sc[:, 1:2],
                                   bias=ln_sc[:, 2:3]),
                      reads=[res_res, r_lnsc], wres=out_wres)

            for db8 in range(8):
                wo_slot = db8 % 4
                kb.dma("sync", A_WO(wo_slot), s_wo[db8], reads=[r_swo],
                       wres=r_wo[wo_slot], sem=f"q_wo{wo_slot}")
                for qs in range(NQ):
                    bank, rbank = ((ps_mmA, r_psA) if qs % 2 == 0
                                   else (ps_mmB, r_psB))
                    kb.mmgroup([(bank[:, 0:256], A_ATT(c)[:, ts(qs, 128)],
                                 A_WO(wo_slot)[:, ts(c, 256)]) for c in range(DC)],
                               rbank,
                               reads=[r_wo[wo_slot]] + [r_att[c] for c in range(DC)])
                    kb.op("vector", lambda e, db8=db8, bank=bank, qs=qs:
                          e.tensor_copy(out=A_RES4(qs)[:, ts(db8, 256)],
                                        in_=bank[:, 0:256]),
                          reads=[rbank], wres=r_res4[qs])
            for qs in range(NQ):
                xs = qs % 2
                kb.dma("scalar", A_RES(xs)[:], x_in[ts(4 * t + qs, 128), :],
                       reads=[], wres=r_res[xs], sem=f"q_xr{xs}")
                kb.op("vector", lambda e, qs=qs, xs=xs: e.tensor_tensor(
                    out=A_RES4(qs)[:], in0=A_RES4(qs), in1=A_RES(xs)[:],
                    op=ALU.add), reads=[r_res[xs], r_res4[qs]], wres=r_res4[qs])
                if dbg and t == 0 and qs == 0:
                    kb.dma("gpsimd", dbg_res, A_RES4(qs)[:],
                           reads=[r_res4[qs]], sem="q_dbg")
                ydst = ybf_ring[:, ts(qs % 2, D)]
                ln_pass(A_RES4(qs), r_res4[qs], ydst, r_ybf[qs % 2])
                kb.dma("gpsimd", s_y[ts(4 * t + qs, 128), :], ydst,
                       reads=[r_ybf[qs % 2]], wres=r_sy[4 * t + qs],
                       sem="q_sy")
                for g4 in range(4):
                    tb, rtb = ((ps_tb1, r_psT1) if g4 % 2 == 0
                               else (ps_tb2, r_psT2))
                    for j in range(4):
                        dc = g4 * 4 + j
                        kb.tmm(tb[:, ts(j, 128)],
                               ybf_ring[:, (qs % 2) * D + dc * 128:
                                        (qs % 2) * D + (dc + 1) * 128],
                               identb[:], rtb, reads=[r_ybf[qs % 2]],
                               ident_res=r_identb)
                    y_dst = bass.AP(xy_fm, (g4 * 4) * TT + qs * 128,
                                    [[DC * TT, 128], [TT, 4], [1, 128]])
                    if g4 % 2 == 0:
                        kb.op("scalar", lambda e, d=y_dst, tb=tb:
                              e.copy(out=d, in_=tb[:, 0:512]),
                              reads=[rtb], wres=r_xyfm)
                    else:
                        kb.op("vector", lambda e, d=y_dst, tb=tb:
                              e.tensor_copy(out=d, in_=tb[:, 0:512]),
                              reads=[rtb], wres=r_xyfm)


            for qs in range(NQ):
                kb.mmgroup([(ps_pv[:, 0:8],
                             xy_fm[:, dc * TT + qs * 128:dc * TT + (qs + 1) * 128],
                             wr_sb[:, ts(dc, 8)]) for dc in range(DC)],
                           r_psPV, reads=[r_xyfm, r_wr])
                RT = lambda j: rt_sb[:, ts(j, 8)]
                S1 = lambda j: rt1_sb[:, j:j + 1]
                kb.op("vector", lambda e: e.tensor_copy(out=RT(0), in_=ps_pv[:, 0:8]),
                      reads=[r_psPV], wres=r_rt)
                kb.op("scalar", lambda e: e.activation(out=RT(1), in_=RT(0),
                                                       func=AF.Exp,
                                                       scale=nrt_sb[:, 0:1]),
                      reads=[r_rt, r_nrt], wres=r_rt)
                kb.op("vector", lambda e: e.tensor_scalar(
                    out=RT(1), in0=RT(1), scalar1=1.0, scalar2=None, op0=ALU.add),
                    reads=[r_rt], wres=r_rt)
                kb.op("vector", lambda e: e.reciprocal(out=RT(2), in_=RT(1)),
                      reads=[r_rt], wres=r_rt)
                kb.op("vector", lambda e: e.tensor_scalar(
                    out=RT(3), in0=RT(2), scalar1=GOLDEN_LOWER, scalar2=None,
                    op0=ALU.is_ge), reads=[r_rt], wres=r_rt)
                kb.op("vector", lambda e: e.tensor_scalar(
                    out=RT(4), in0=RT(2), scalar1=GOLDEN_UPPER, scalar2=None,
                    op0=ALU.is_le), reads=[r_rt], wres=r_rt)
                kb.op("vector", lambda e: e.tensor_tensor(
                    out=RT(3), in0=RT(3), in1=RT(4), op=ALU.mult),
                    reads=[r_rt], wres=r_rt)
                kb.op("vector", lambda e: e.tensor_scalar(
                    out=RT(5), in0=RT(2), scalar1=GOLDEN_CENTER, scalar2=None,
                    op0=ALU.subtract), reads=[r_rt], wres=r_rt)
                kb.op("scalar", lambda e: e.activation(
                    out=RT(5), in_=RT(5), func=AF.Abs), reads=[r_rt], wres=r_rt)
                kb.op("vector", lambda e: e.tensor_scalar(
                    out=RT(6), in0=RT(5), scalar1=-10.0, scalar2=None, op0=ALU.mult),
                    reads=[r_rt], wres=r_rt)
                kb.op("scalar", lambda e: e.activation(out=RT(6), in_=RT(6),
                                                       func=AF.Exp),
                      reads=[r_rt], wres=r_rt)
                kb.op("vector", lambda e: e.tensor_tensor(
                    out=RT(6), in0=RT(6), in1=RT(3), op=ALU.mult),
                    reads=[r_rt], wres=r_rt)
                kb.op("vector", lambda e: e.reduce_sum(out=S1(0), in_=RT(6),
                                                       axis=AX.X),
                      reads=[r_rt], wres=r_rt1)
                kb.op("vector", lambda e: e.tensor_scalar(
                    out=S1(1), in0=S1(0), scalar1=1e-8, scalar2=None, op0=ALU.is_lt),
                    reads=[r_rt1], wres=r_rt1)
                kb.op("vector", lambda e: e.tensor_scalar(
                    out=S1(2), in0=S1(1), scalar1=-1.0, scalar2=1.0,
                    op0=ALU.mult, op1=ALU.add), reads=[r_rt1], wres=r_rt1)
                kb.op("vector", lambda e: e.tensor_scalar(
                    out=RT(7), in0=RT(5), scalar1=-1.0 / 0.3, scalar2=None,
                    op0=ALU.mult), reads=[r_rt], wres=r_rt)
                kb.op("scalar", lambda e: e.activation(out=RT(7), in_=RT(7),
                                                       func=AF.Exp),
                      reads=[r_rt], wres=r_rt)
                kb.op("vector", lambda e: e.reduce_max(out=S1(3), in_=RT(7),
                                                       axis=AX.X),
                      reads=[r_rt], wres=r_rt1)
                kb.op("vector", lambda e: e.tensor_scalar(
                    out=RT(8), in0=RT(7), scalar1=S1(3), scalar2=None, op0=ALU.is_ge),
                    reads=[r_rt, r_rt1], wres=r_rt)
                kb.op("vector", lambda e: e.tensor_scalar(
                    out=RT(9), in0=RT(8), scalar1=-1.0, scalar2=1.0,
                    op0=ALU.mult, op1=ALU.add), reads=[r_rt], wres=r_rt)
                kb.op("vector", lambda e: e.tensor_tensor(
                    out=RT(9), in0=RT(7), in1=RT(9), op=ALU.mult),
                    reads=[r_rt], wres=r_rt)
                kb.op("vector", lambda e: e.reduce_max(out=S1(4), in_=RT(9),
                                                       axis=AX.X),
                      reads=[r_rt], wres=r_rt1)
                kb.op("vector", lambda e: e.tensor_scalar(
                    out=RT(10), in0=RT(9), scalar1=S1(4), scalar2=None, op0=ALU.is_ge),
                    reads=[r_rt, r_rt1], wres=r_rt)
                kb.op("vector", lambda e: e.tensor_tensor(
                    out=RT(10), in0=RT(10), in1=RT(8), op=ALU.add),
                    reads=[r_rt], wres=r_rt)
                kb.op("vector", lambda e: e.tensor_tensor(
                    out=RT(10), in0=RT(10), in1=RT(7), op=ALU.mult),
                    reads=[r_rt], wres=r_rt)
                kb.op("vector", lambda e: e.reduce_sum(out=S1(5), in_=RT(10),
                                                       axis=AX.X),
                      reads=[r_rt], wres=r_rt1)
                kb.op("vector", lambda e: e.tensor_scalar(
                    out=S1(5), in0=S1(5), scalar1=1e-8, scalar2=None, op0=ALU.max),
                    reads=[r_rt1], wres=r_rt1)
                kb.op("vector", lambda e: e.reciprocal(out=S1(6), in_=S1(5)),
                      reads=[r_rt1], wres=r_rt1)
                kb.op("vector", lambda e: e.tensor_scalar(
                    out=RT(10), in0=RT(10), scalar1=S1(6), scalar2=S1(1),
                    op0=ALU.mult, op1=ALU.mult), reads=[r_rt, r_rt1], wres=r_rt)
                kb.op("vector", lambda e: e.tensor_scalar(
                    out=RT(11), in0=RT(6), scalar1=S1(2), scalar2=None, op0=ALU.mult),
                    reads=[r_rt, r_rt1], wres=r_rt)
                kb.op("vector", lambda e: e.tensor_tensor(
                    out=RT(11), in0=RT(11), in1=RT(10), op=ALU.add),
                    reads=[r_rt], wres=r_rt)
                kb.op("vector", lambda e: e.reduce_sum(out=S1(7), in_=RT(11),
                                                       axis=AX.X),
                      reads=[r_rt], wres=r_rt1)
                kb.op("vector", lambda e: e.tensor_scalar(
                    out=S1(7), in0=S1(7), scalar1=1e-8, scalar2=None, op0=ALU.max),
                    reads=[r_rt1], wres=r_rt1)
                kb.op("vector", lambda e: e.reciprocal(out=S1(0), in_=S1(7)),
                      reads=[r_rt1], wres=r_rt1)
                kb.op("vector", lambda e, qs=qs: e.tensor_scalar(
                    out=w_tm[:, ts(qs, E)], in0=RT(11), scalar1=S1(0), scalar2=None,
                    op0=ALU.mult), reads=[r_rt, r_rt1], wres=r_wtm)

            seed(r_wgu + r_wd + r_mid,
                 r_wo + r_att + [r_qbf, r_psb] + r_PT + r_wq + r_Prow)
            seed(r_acc, r_res4)
            for e_ in range(E):
                for qs in range(NQ):
                    kb.transpose(ps_tf[0:1, 256:384],
                                 w_tm[:, qs * E + e_:qs * E + e_ + 1], identf[:],
                                 r_psTF, reads=[r_wtm], ident_res=r_identf)
                    kb.op("scalar", lambda e, qs=qs: e.copy(
                        out=rrow_sb[0:1, ts(qs, 128)], in_=ps_tf[0:1, 256:384]),
                        reads=[r_psTF], wres=r_rrow)
                kb.mmgroup([(ps_bc[:], onesb[:], rrow_sb[0:1, 0:512])],
                           r_psBC, reads=[r_onesb, r_rrow])
                kb.op("scalar", lambda e: e.copy(out=wb_sb[:], in_=ps_bc[:]),
                      reads=[r_psBC], wres=r_wb)

                if e_ < NL:
                    l = e_
                    for nm in ("g", "u"):
                        asrc = ga_sb if nm == "g" else ua_sb
                        ares_src = r_ga if nm == "g" else r_ua
                        adst = a_fm if nm == "g" else au_fm
                        ares = r_afm if nm == "g" else r_aufm
                        for qs in range(NQ):
                            kb.mmgroup([(ps_pv[:, 0:64],
                                         xy_fm[:, dc * TT + qs * 128:
                                               dc * TT + (qs + 1) * 128],
                                         asrc[:, ts(l * DC + dc, R)])
                                        for dc in range(DC)], r_psPV,
                                       reads=[r_xyfm, ares_src])
                            kb.op("scalar", lambda e: e.copy(out=aev_sb[:, 0:64],
                                                             in_=ps_pv[:, 0:64]),
                                  reads=[r_psPV], wres=r_aev)
                            kb.tmm(ps_tb1[0:64, 0:128], aev_sb[:, 0:64],
                                   identb[:], r_psT1, reads=[r_aev],
                                   ident_res=r_identb)
                            kb.op("scalar", lambda e, qs=qs, adst=adst: e.copy(
                                out=adst[0:64, ts(qs, 128)], in_=ps_tb1[0:64, 0:128]),
                                reads=[r_psT1], wres=ares)
                            kb.op("scalar", lambda e, qs=qs, adst=adst: e.copy(
                                out=adst[64:128, ts(qs, 128)], in_=ps_tb1[0:64, 0:128]),
                                reads=[r_psT1], wres=ares)

                for fb in range(8):
                    gu_g = fb % 2
                    gu_u = 2 + fb % 2
                    kb.dma("sync", A_WGU(gu_g), s_wg[e_, fb], reads=[r_swg[e_]],
                           wres=r_wgu[gu_g], sem=f"q_gu{gu_g}")
                    kb.dma("sync", A_WGU(gu_u), s_wu[e_, fb], reads=[r_swu[e_]],
                           wres=r_wgu[gu_u], sem=f"q_gu{gu_u}")
                    for fs in range(2):
                        fc = 2 * fb + fs
                        gbank, rgbank = ((ps_mmA, r_psA) if fc % 2 == 0
                                         else (ps_tf, r_psTF))
                        mms = [(gbank[:], A_WGU(gu_g)[:, dc * 256 + fs * 128:
                                                      dc * 256 + fs * 128 + 128],
                                xy_fm[:, ts(dc, TT)]) for dc in range(DC)]
                        rds = [r_wgu[gu_g], r_xyfm]
                        if e_ < NL:
                            mms.append((gbank[:],
                                        gb_sb[64 * e_:64 * e_ + 64,
                                              fc * 128:(fc + 1) * 128],
                                        a_fm[64 * e_:64 * e_ + 64, :]))
                            rds += [r_gb, r_afm]
                        kb.mmgroup(mms, rgbank, reads=rds)
                        sgs = fc % 2
                        kb.op("scalar", lambda e, sgs=sgs, gbank=gbank:
                              e.activation(out=sg_ring[:, ts(sgs, 512)],
                                           in_=gbank[:], func=AF.Silu),
                              reads=[rgbank], wres=r_sg[sgs])
                        ubank, rubank = ((ps_mmB, r_psB) if fc % 2 == 0
                                         else (ps_bc, r_psBC))
                        mms = [(ubank[:], A_WGU(gu_u)[:, dc * 256 + fs * 128:
                                                      dc * 256 + fs * 128 + 128],
                                xy_fm[:, ts(dc, TT)]) for dc in range(DC)]
                        rds = [r_wgu[gu_u], r_xyfm]
                        if e_ < NL:
                            mms.append((ubank[:],
                                        ub_sb[64 * e_:64 * e_ + 64,
                                              fc * 128:(fc + 1) * 128],
                                        au_fm[64 * e_:64 * e_ + 64, :]))
                            rds += [r_ub, r_aufm]
                        kb.mmgroup(mms, rubank, reads=rds)
                        kb.op("vector", lambda e, fc=fc, sgs=sgs, ubank=ubank:
                              e.tensor_tensor(out=A_MID(fc), in0=ubank[:],
                                              in1=sg_ring[:, ts(sgs, 512)],
                                              op=ALU.mult),
                              reads=[rubank, r_sg[sgs]], wres=r_mid[fc])

                if e_ < NL:
                    l = e_
                    kb.mmgroup([(ps_pv[0:64, :],
                                 da_sb[:, ts(l * FC + fc, R)], A_MID(fc))
                                for fc in range(FC)], r_psPV,
                               reads=[r_da] + [r_mid[fc] for fc in range(FC)])
                    kb.op("scalar", lambda e: e.copy(out=am_fm[0:64, :],
                                                     in_=ps_pv[0:64, :]),
                          reads=[r_psPV], wres=r_amfm)
                    kb.op("scalar", lambda e: e.copy(out=am_fm[64:128, :],
                                                     in_=ps_pv[0:64, :]),
                          reads=[r_psPV], wres=r_amfm)

                for db2 in range(8):
                    wd_slot = db2 % 2
                    kb.dma("sync", A_WD(wd_slot), s_wd[e_, db2], reads=[r_swd[e_]],
                           wres=r_wd[wd_slot], sem=f"q_wd{wd_slot}")
                    for ds in range(2):
                        dchunk = 2 * db2 + ds
                        dbank, rdbank = ((ps_mmC, r_psC) if dchunk % 2 == 0
                                         else (ps_pv, r_psPV))
                        mms = [(dbank[:],
                                A_WD(wd_slot)[:, fc * 256 + ds * 128:
                                              fc * 256 + ds * 128 + 128],
                                A_MID(fc)) for fc in range(FC)]
                        rds = [r_wd[wd_slot]] + [r_mid[fc] for fc in range(FC)]
                        if e_ < NL:
                            mms.append((dbank[:],
                                        db_sb[64 * e_:64 * e_ + 64,
                                              dchunk * 128:(dchunk + 1) * 128],
                                        am_fm[64 * e_:64 * e_ + 64, :]))
                            rds += [r_db, r_amfm]
                        kb.mmgroup(mms, rdbank, reads=rds)
                        if e_ == 0:
                            kb.op("vector", lambda e, dchunk=dchunk, dbank=dbank:
                                  e.tensor_tensor(out=A_ACC(dchunk), in0=dbank[:],
                                                  in1=wb_sb[:], op=ALU.mult),
                                  reads=[rdbank, r_wb], wres=r_acc[dchunk])
                        else:
                            t1s = dchunk % 2
                            kb.op("vector", lambda e, dbank=dbank, t1s=t1s:
                                  e.tensor_tensor(out=t1_ring[:, ts(t1s, 512)],
                                                  in0=dbank[:], in1=wb_sb[:],
                                                  op=ALU.mult),
                                  reads=[rdbank, r_wb], wres=r_t1[t1s])
                            kb.op("vector", lambda e, dchunk=dchunk, t1s=t1s:
                                  e.tensor_tensor(out=A_ACC(dchunk),
                                                  in0=A_ACC(dchunk),
                                                  in1=t1_ring[:, ts(t1s, 512)],
                                                  op=ALU.add),
                                  reads=[r_t1[t1s], r_acc[dchunk]],
                                  wres=r_acc[dchunk])

            for qs in range(NQ):
                y2s = qs % 2
                kb.dma("scalar", ybf_ring[:, ts(y2s, D)], s_y[ts(4 * t + qs, 128), :],
                       reads=[r_sy[4 * t + qs]], wres=r_ybf[y2s],
                       sem=f"q_yb{y2s}")
                res_slot = qs % 2
                for dc in range(DC):
                    kb.transpose(ps_tf[:, ts(dc % 4, 128)],
                                 A_ACC(dc)[:, ts(qs, 128)], identf[:],
                                 r_psTF, reads=[r_acc[dc]], ident_res=r_identf)
                    kb.op("vector", lambda e, dc=dc, y2s=y2s, res_slot=res_slot:
                          e.tensor_tensor(
                        out=A_RES(res_slot)[:, ts(dc, 128)],
                        in0=ps_tf[:, ts(dc % 4, 128)],
                        in1=ybf_ring[:, y2s * D + dc * 128:y2s * D + (dc + 1) * 128],
                        op=ALU.add),
                        reads=[r_psTF, r_ybf[y2s]], wres=r_res[res_slot])
                out_slot = (qs + 1) % 2
                ln_pass(A_RES(res_slot), r_res[res_slot], A_RES(out_slot)[:],
                        r_res[out_slot])
                kb.dma("gpsimd", out_ap[ts(4 * t + qs, 128), :],
                       A_RES(out_slot)[:], reads=[r_res[out_slot]], wres=r_out,
                       sem="q_out")

        kb.emit("gpsimd", lambda e: None,
                [(s, kb.cnt[s]) for s in DMA_SEMS if kb.cnt[s] > 0])

        block = es.enter_context(nc.Block())

        def runner(eng):
            def run(engobj):
                with nc.allow_low_precision(
                        reason="bf16 matmul operands with fp32 accumulation are "
                               "the intended precision policy for this kernel"):
                    for fn, waits, inc, inc_n in kb.ops[eng]:
                        for s, v in waits:
                            engobj.wait_ge(sems[s], v)
                        ins = fn(engobj)
                        if inc is not None and ins is not None:
                            ins.then_inc(sems[inc], inc_n)
            return run

        block.tensor(runner("tensor"))
        block.vector(runner("vector"))
        block.scalar(runner("scalar"))
        block.sync(runner("sync"))
        block.gpsimd(runner("gpsimd"))

    return nc


def _consts():
    ident = np.eye(128, dtype=np.float32)
    mask = np.where(np.tril(np.ones((128, 128), dtype=bool)), 0.0,
                    -1e9).astype(np.float32)
    theta = 1.0 / 10000.0 ** (np.arange(0, 32, dtype=np.float64) / 32.0)
    cc = np.zeros((D,), dtype=np.float64)
    ss = np.zeros((D,), dtype=np.float64)
    for h in range(H):
        for p in range(64):
            ang = h * theta[p % 32]
            cc[128 * h + 2 * p] = math.cos(ang)
            cc[128 * h + 2 * p + 1] = math.cos(ang)
            ss[128 * h + 2 * p] = -math.sin(ang)
            ss[128 * h + 2 * p + 1] = math.sin(ang)
    cc *= SCALING
    ss *= SCALING
    return {
        "cc_identb": ident.astype(ml_dtypes.bfloat16),
        "cc_identf": ident,
        "cc_mask": mask,
        "cc_ropec": np.broadcast_to(cc.astype(np.float32), (128, D)).copy(),
        "cc_ropes": np.broadcast_to(ss.astype(np.float32), (128, D)).copy(),
        "cc_onesb": np.ones((1, 128), dtype=np.float32).astype(ml_dtypes.bfloat16),
        "cc_onesf": np.ones((1, 128), dtype=np.float32),
    }


def _prep_weights(i):
    c = np.ascontiguousarray
    f32 = lambda a: np.asarray(a, dtype=np.float32)
    wq = f32(i["L0_attn_Wq"])
    wk = f32(i["L0_attn_Wk"])
    wv = f32(i["L0_attn_Wv"])
    wo = f32(i["L0_attn_Wo"])
    wg = f32(i["L0_ffn_Wg"])
    wu = f32(i["L0_ffn_Wu"])
    wd = f32(i["L0_ffn_Wd"])
    wkv_pm = np.stack([
        wk.T.reshape(16, 128, 128).transpose(1, 0, 2).reshape(128, DC * 128),
        wv.T.reshape(16, 128, 128).transpose(1, 0, 2).reshape(128, DC * 128)])
    wo_b = wo.T.reshape(16, 128, 8, 256).transpose(2, 1, 0, 3).reshape(8, 128, DC * 256)
    blk = lambda w: (w.reshape(8, 16, 128, 8, 256).transpose(0, 3, 2, 1, 4)
                     .reshape(8, 8, 128, 16 * 256))
    return {
        "wqT": c(wq.T),
        "wkv_pm": c(wkv_pm),
        "wo_blk": c(wo_b),
        "wg_blk": c(blk(wg)),
        "wu_blk": c(blk(wu)),
        "wd_blk": c(blk(wd)),
        "ga": f32(i["L0_ffn_Ga"]),
        "gb": f32(i["L0_ffn_Gb"]),
        "ua": f32(i["L0_ffn_Ua"]),
        "ub": f32(i["L0_ffn_Ub"]),
        "da": f32(i["L0_ffn_Da"]),
        "db": f32(i["L0_ffn_Db"]),
        "wr": f32(i["L0_ffn_Wr"]),
        "temp": f32(i["L0_ffn_temperature"]).reshape(1, 1),
    }


def kernel(**inputs):
    x = np.asarray(inputs["L0_attn_hidden_states"], dtype=np.float32)
    wm = _prep_weights(inputs)
    wm.update(_consts())
    nc = build_nc()
    in_maps = []
    for b in range(B):
        m = dict(wm)
        m["x"] = np.ascontiguousarray(x[b])
        in_maps.append(m)
    res = run_bass_kernel_spmd(nc, in_maps, core_ids=list(range(B)))
    return np.stack([r["out"] for r in res.results], axis=0)


# version 11
# speedup vs baseline: 1.0370x; 1.0370x over previous
import math
from contextlib import ExitStack

import numpy as np
import ml_dtypes

import concourse.bass as bass
import concourse.mybir as mybir
from concourse.bass_utils import run_bass_kernel_spmd

F32 = mybir.dt.float32
BF16 = mybir.dt.bfloat16
AF = mybir.ActivationFunctionType
ALU = mybir.AluOpType
AX = mybir.AxisListType

B, T, D = 8, 4096, 2048
H, DH = 16, 128
E, F, R, NL = 8, 2048, 64, 2
NT = 8
TT = 512
NQ = 4
DC = 16
FC = 16
EPS = 1e-5
GOLDEN_CENTER = 1.0 / math.e
GOLDEN_LOWER = 0.5 - math.log(4.0 / 3.0)
GOLDEN_UPPER = 0.5
SCALING = DH ** (-0.5)

ENGINES = ("tensor", "vector", "scalar", "sync", "gpsimd")
ESEM = {"tensor": "pe", "vector": "dve", "scalar": "act"}
DMA_SEMS = (["q_cn", "q_stg", "q_wkv", "q_ctw", "q_swq", "q_xy", "q_ctm", "q_sxb",
             "q_sy", "q_out", "q_dbg"]
            + [f"q_wq{i}" for i in range(2)] + [f"q_wo{i}" for i in range(4)]
            + [f"q_gu{i}" for i in range(4)] + [f"q_wd{i}" for i in range(2)]
            + [f"q_xr{i}" for i in range(2)] + [f"q_yb{i}" for i in range(2)]
            + [f"q_ct{i}" for i in range(8)])
ALL_SEMS = ["pe", "dve", "act"] + DMA_SEMS


class Res:
    __slots__ = ("w", "rs")

    def __init__(self):
        self.w = None
        self.rs = []

    def for_write(self):
        d = list(self.rs)
        if self.w is not None:
            d.append(self.w)
        return d

    def for_read(self):
        return [self.w] if self.w is not None else []

    def all_evs(self):
        return self.for_write()

    def set_w(self, ev):
        self.w = ev
        self.rs = []

    def add_r(self, ev):
        self.rs.append(ev)


def seed(new_reses, old_reses):
    evs = []
    for o in old_reses:
        evs += o.all_evs()
    for n in new_reses:
        n.rs.extend(evs)


class KB:
    def __init__(self, sems):
        self.sems = sems
        self.ops = {e: [] for e in ENGINES}
        self.cnt = {s: 0 for s in ALL_SEMS}
        self.lastw = {}

    def emit(self, eng, fn, deps=(), inc=None, inc_n=1):
        seen = {}
        for d in deps:
            if d is None:
                continue
            s, v = d
            if v <= 0:
                continue
            if s.startswith("q_"):
                v = self.cnt[s]
                if v <= 0:
                    continue
            if seen.get(s, 0) < v:
                seen[s] = v
        waits = []
        for s, v in seen.items():
            if eng == "tensor" and s == "pe":
                continue
            if self.lastw.get((eng, s), 0) >= v:
                continue
            self.lastw[(eng, s)] = v
            waits.append((s, v))
        ev = None
        if inc is not None:
            self.cnt[inc] += inc_n
            ev = (inc, self.cnt[inc])
        self.ops[eng].append((fn, waits, inc, inc_n))
        return ev

    def dma(self, q, out, in_, reads=(), wres=None, tp=False, sem="q_cn"):
        deps = list(wres.for_write()) if wres is not None else []
        for r in reads:
            deps += r.for_read()
        if tp:
            fn = lambda e, o=out, i=in_: e.dma_start_transpose(out=o, in_=i)
        else:
            fn = lambda e, o=out, i=in_: e.dma_start(out=o, in_=i)
        ev = self.emit(q, fn, deps, inc=sem, inc_n=16)
        if wres is not None:
            wres.set_w(ev)
        for r in reads:
            r.add_r(ev)
        return ev

    def mmgroup(self, mms, bank, reads=()):
        deps = list(bank.for_write())
        for r in reads:
            deps += r.for_read()
        n = len(mms)
        ev = None
        for i, (o, l, r_) in enumerate(mms):
            st, sp = (i == 0), (i == n - 1)
            fn = (lambda e, o=o, l=l, r_=r_, st=st, sp=sp:
                  e.matmul(o, l, r_, start=st, stop=sp))
            ev = self.emit("tensor", fn, deps if i == 0 else (),
                           inc="pe" if sp else None)
        bank.set_w(ev)
        for r in reads:
            r.add_r(ev)
        return ev

    def tmm(self, out, in_, ident, bank, reads=(), ident_res=None):
        deps = list(bank.for_write())
        for r in reads:
            deps += r.for_read()
        if ident_res is not None:
            deps += ident_res.for_read()
        fn = lambda e, o=out, i=in_, d=ident: e.matmul(o, i, d, start=True,
                                                       stop=True)
        ev = self.emit("tensor", fn, deps, inc="pe")
        bank.set_w(ev)
        for r in reads:
            r.add_r(ev)
        if ident_res is not None:
            ident_res.add_r(ev)
        return ev

    def transpose(self, out, in_, ident, bank, reads=(), ident_res=None):
        deps = list(bank.for_write())
        for r in reads:
            deps += r.for_read()
        if ident_res is not None:
            deps += ident_res.for_read()
        fn = lambda e, o=out, i=in_, d=ident: e.transpose(o, i, d)
        ev = self.emit("tensor", fn, deps, inc="pe")
        bank.set_w(ev)
        for r in reads:
            r.add_r(ev)
        if ident_res is not None:
            ident_res.add_r(ev)
        return ev

    def op(self, eng, fn, reads=(), wres=None, wres2=None):
        wl = []
        if wres is not None:
            wl = list(wres) if isinstance(wres, list) else [wres]
        if wres2 is not None:
            wl.append(wres2)
        deps = []
        for w in wl:
            deps += w.for_write()
        for r in reads:
            deps += r.for_read()
        ev = self.emit(eng, fn, deps, inc=ESEM[eng])
        for w in wl:
            w.set_w(ev)
        for r in reads:
            r.add_r(ev)
        return ev


def ts(i, n):
    return slice(i * n, (i + 1) * n)


def build_nc(num_tiles=NT, dbg=False):
    nc = bass.Bass("TRN2", target_bir_lowering=False)
    skind = {"kind": "ExternalOutput"} if dbg else {}

    x_in = nc.dram_tensor("x", [T, D], F32, kind="ExternalInput").ap()
    wqT = nc.dram_tensor("wqT", [D, D], F32, kind="ExternalInput").ap()
    wkv_pm = nc.dram_tensor("wkv_pm", [2, 128, DC * 128], F32, kind="ExternalInput").ap()
    wo_blk = nc.dram_tensor("wo_blk", [8, 128, DC * 256], F32, kind="ExternalInput").ap()
    wg_blk = nc.dram_tensor("wg_blk", [E, 8, 128, DC * 256], F32, kind="ExternalInput").ap()
    wu_blk = nc.dram_tensor("wu_blk", [E, 8, 128, DC * 256], F32, kind="ExternalInput").ap()
    wd_blk = nc.dram_tensor("wd_blk", [E, 8, 128, FC * 256], F32, kind="ExternalInput").ap()
    ga_in = nc.dram_tensor("ga", [NL, D, R], F32, kind="ExternalInput").ap()
    gb_in = nc.dram_tensor("gb", [NL, R, F], F32, kind="ExternalInput").ap()
    ua_in = nc.dram_tensor("ua", [NL, D, R], F32, kind="ExternalInput").ap()
    ub_in = nc.dram_tensor("ub", [NL, R, F], F32, kind="ExternalInput").ap()
    da_in = nc.dram_tensor("da", [NL, F, R], F32, kind="ExternalInput").ap()
    db_in = nc.dram_tensor("db", [NL, R, D], F32, kind="ExternalInput").ap()
    wr_in = nc.dram_tensor("wr", [D, E], F32, kind="ExternalInput").ap()
    temp_in = nc.dram_tensor("temp", [1, 1], F32, kind="ExternalInput").ap()
    cc_identb = nc.dram_tensor("cc_identb", [128, 128], BF16, kind="ExternalInput").ap()
    cc_identf = nc.dram_tensor("cc_identf", [128, 128], F32, kind="ExternalInput").ap()
    cc_mask = nc.dram_tensor("cc_mask", [128, 128], F32, kind="ExternalInput").ap()
    cc_ropec = nc.dram_tensor("cc_ropec", [128, D], F32, kind="ExternalInput").ap()
    cc_ropes = nc.dram_tensor("cc_ropes", [128, D], F32, kind="ExternalInput").ap()
    cc_onesb = nc.dram_tensor("cc_onesb", [1, 128], BF16, kind="ExternalInput").ap()
    cc_onesf = nc.dram_tensor("cc_onesf", [1, 128], F32, kind="ExternalInput").ap()

    out_ap = nc.dram_tensor("out", [T, D], F32, kind="ExternalOutput").ap()
    if dbg:
        dbg_k = nc.dram_tensor("dbg_k", [128, T], BF16, kind="ExternalOutput").ap()
        dbg_v = nc.dram_tensor("dbg_v", [128, T], BF16, kind="ExternalOutput").ap()
        dbg_q = nc.dram_tensor("dbg_q", [128, TT], BF16, kind="ExternalOutput").ap()
        dbg_S = nc.dram_tensor("dbg_S", [128, 4096], F32, kind="ExternalOutput").ap()
        dbg_P = nc.dram_tensor("dbg_P", [128, 4096], BF16, kind="ExternalOutput").ap()
        dbg_att = nc.dram_tensor("dbg_att", [H, 128, TT], BF16, kind="ExternalOutput").ap()
        dbg_res = nc.dram_tensor("dbg_res", [128, D], F32, kind="ExternalOutput").ap()
        dbg_xfm = nc.dram_tensor("dbg_xfm", [128, DC * TT], BF16, kind="ExternalOutput").ap()
        dbg_rsum = nc.dram_tensor("dbg_rsum", [128, NQ], F32, kind="ExternalOutput").ap()
        dbg_bc = nc.dram_tensor("dbg_bc", [128, 256], F32, kind="ExternalOutput").ap()
        dbg_pt = nc.dram_tensor("dbg_pt", [4, 128, 256], BF16, kind="ExternalOutput").ap()

    s_xbf = nc.dram_tensor("s_xbf", [T, D], BF16, **skind).ap()
    s_y = nc.dram_tensor("s_y", [T, D], BF16, **skind).ap()
    s_wq = nc.dram_tensor("s_wq", [H, 128, DC * 128], BF16, **skind).ap()
    s_wo = nc.dram_tensor("s_wo", [8, 128, DC * 256], BF16).ap()
    s_wg = nc.dram_tensor("s_wg", [E, 8, 128, DC * 256], BF16).ap()
    s_wu = nc.dram_tensor("s_wu", [E, 8, 128, DC * 256], BF16).ap()
    s_wd = nc.dram_tensor("s_wd", [E, 8, 128, FC * 256], BF16).ap()
    s_wkv = nc.dram_tensor("s_wkv", [2, 128, DC * 128], BF16).ap()

    es = ExitStack()
    sb = lambda name, shape, dt: es.enter_context(nc.sbuf_tensor(name, shape, dt))
    psum = lambda name, shape, dt: es.enter_context(nc.psum_tensor(name, shape, dt))

    identb = sb("identb", [128, 128], BF16)
    identf = sb("identf", [128, 128], F32)
    mask_sb = sb("mask_sb", [128, 128], F32)
    onesb = sb("onesb", [1, 128], BF16)
    onesf = sb("onesf", [1, 128], F32)
    wr_sb = sb("wr_sb", [128, DC * E], BF16)
    nrt_sb = sb("nrt_sb", [128, 1], F32)
    t_tmp = sb("t_tmp", [1, 2], F32)
    ga_sb = sb("ga_sb", [128, NL * DC * R], BF16)
    ua_sb = sb("ua_sb", [128, NL * DC * R], BF16)
    da_sb = sb("da_sb", [128, NL * FC * R], BF16)
    gb_sb = sb("gb_sb", [128, F], BF16)
    ub_sb = sb("ub_sb", [128, F], BF16)
    db_sb = sb("db_sb", [128, D], BF16)
    k_fm = sb("k_fm", [128, T], BF16)
    v_tm = sb("v_tm", [128, T], BF16)
    wkv_sb = sb("wkv_sb", [128, 2 * DC * 128], BF16)
    xy_fm = sb("xy_fm", [128, DC * TT], BF16)
    arena_bf = sb("arena_bf", [128, 32768], BF16)
    arena_f = sb("arena_f", [128, 12288], F32)
    x_ring = sb("x_ring", [128, 2 * 512], F32)
    ybf_ring = sb("ybf_ring", [128, 2 * D], BF16)
    sg_ring = sb("sg_ring", [128, 2 * 512], BF16)
    t1_ring = sb("t1_ring", [128, 2 * 512], F32)
    wb_sb = sb("wb_sb", [128, 512], F32)
    bcast_sb = sb("bcast_sb", [128, 256], F32)
    a_fm = sb("a_fm", [128, TT], BF16)
    au_fm = sb("au_fm", [128, TT], BF16)
    am_fm = sb("am_fm", [128, TT], BF16)
    aev_sb = sb("aev_sb", [128, 64], BF16)
    srow_sb = sb("srow_sb", [1, 256], F32)
    rrow_sb = sb("rrow_sb", [1, 512], BF16)
    negmax_sb = sb("negmax_sb", [128, NQ], F32)
    rsum_sb = sb("rsum_sb", [128, NQ], F32)
    sm2_sb = sb("sm2_sb", [128, 2], F32)
    smax_sb = sb("smax_sb", [128, NQ * 8], F32)
    bnt_sb = sb("bnt_sb", [128, 24], F32)
    mv_sb = sb("mv_sb", [128, 2], F32)
    ln_sc = sb("ln_sc", [128, 4], F32)
    rt_sb = sb("rt_sb", [128, 12 * E], F32)
    rt1_sb = sb("rt1_sb", [128, 8], F32)
    w_tm = sb("w_tm", [128, NQ * E], F32)

    ps_mmA = psum("ps_mmA", [128, 512], F32)
    ps_mmB = psum("ps_mmB", [128, 512], F32)
    ps_mmC = psum("ps_mmC", [128, 512], F32)
    ps_pv = psum("ps_pv", [128, 512], F32)
    ps_tb1 = psum("ps_tb1", [128, 512], F32)
    ps_tb2 = psum("ps_tb2", [128, 512], F32)
    ps_tf = psum("ps_tf", [128, 512], F32)
    ps_bc = psum("ps_bc", [128, 512], F32)

    sems = {}
    for s in ALL_SEMS:
        sems[s] = es.enter_context(nc.semaphore("sem_" + s))
    kb = KB(sems)

    R_ = Res
    r_identb, r_identf, r_mask, r_onesb, r_onesf = R_(), R_(), R_(), R_(), R_()
    r_wr, r_nrt, r_ttmp = R_(), R_(), R_()
    r_ga, r_ua, r_da, r_gb, r_ub, r_db = R_(), R_(), R_(), R_(), R_(), R_()
    r_kfm, r_vtm, r_wkv = R_(), R_(), R_()
    r_xyfm = R_()
    r_xring = [R_() for _ in range(2)]
    r_ybf = [R_() for _ in range(2)]
    r_sg = [R_() for _ in range(2)]
    r_t1 = [R_() for _ in range(2)]
    r_wb, r_bcast = R_(), R_()
    r_afm, r_aufm, r_amfm, r_aev = R_(), R_(), R_(), R_()
    r_srow, r_rrow = R_(), R_()
    r_negmax, r_rsum, r_sm2, r_smax = R_(), R_(), R_(), R_()
    r_bnt, r_mv, r_lnsc = R_(), R_(), R_()
    r_rt, r_rt1, r_wtm = R_(), R_(), R_()
    r_psA, r_psB, r_psC, r_psPV = R_(), R_(), R_(), R_()
    r_psT1, r_psT2, r_psTF, r_psBC = R_(), R_(), R_(), R_()
    r_sxbf = [R_() for _ in range(32)]
    r_sy = [R_() for _ in range(32)]
    r_swq, r_swo, r_swkv = R_(), R_(), R_()
    r_swg = [R_() for _ in range(E)]
    r_swu = [R_() for _ in range(E)]
    r_swd = [R_() for _ in range(E)]
    r_out = R_()

    A_PT = lambda kc: arena_bf[:, kc * 256:(kc + 1) * 256]
    A_PSB = arena_bf[:, 8192:12288]
    A_WQ = lambda j: arena_bf[:, 12288 + j * 2048:12288 + (j + 1) * 2048]
    A_QBF = arena_bf[:, 16384:16896]
    A_ATT = lambda h: arena_bf[:, 17408 + h * 512:17408 + (h + 1) * 512]
    A_WO = lambda j: arena_bf[:, j * 4096:(j + 1) * 4096]
    A_WGU = lambda j: arena_bf[:, j * 4096:(j + 1) * 4096]
    A_WD = lambda j: arena_bf[:, 16384 + j * 4096:16384 + (j + 1) * 4096]
    A_MID = lambda fc: arena_bf[:, 24576 + fc * 512:24576 + (fc + 1) * 512]
    r_PT = [R_() for _ in range(32)]
    r_psb, r_qbf = R_(), R_()
    r_wq = [R_() for _ in range(2)]
    r_att = [R_() for _ in range(H)]
    r_wo = [R_() for _ in range(4)]
    r_wgu = [R_() for _ in range(4)]
    r_wd = [R_() for _ in range(2)]
    r_mid = [R_() for _ in range(FC)]

    A_ACC = lambda dc: arena_f[:, dc * 512:(dc + 1) * 512]
    A_RES4 = lambda qs: arena_f[:, qs * 2048:(qs + 1) * 2048]
    A_S = arena_f[:, 8192:12288]
    A_SROW = lambda r: arena_f[:, r * 4096:(r + 1) * 4096]
    A_PROW = (arena_bf[:, 8192:12288], arena_bf[:, 25600:29696])
    A_RES = lambda j: arena_f[:, 8192 + j * 2048:8192 + (j + 1) * 2048]
    P_CC = arena_f[:, 0:2048]
    P_SS = arena_f[:, 2048:4096]
    P_STG = arena_f[:, 4096:6144]
    P_ROT = arena_f[:, 6144:8192]
    r_acc = [R_() for _ in range(DC)]
    r_res4 = [R_() for _ in range(NQ)]
    r_S = R_()
    r_Sb = [R_() for _ in range(8)]
    r_Srow = [[R_() for _ in range(8)] for _ in range(3)]
    r_Prow = [R_(), R_()]
    r_res = [R_() for _ in range(2)]
    r_pcc, r_pss, r_pstg, r_prot, r_ptmp = R_(), R_(), R_(), R_(), R_()
    r_wqstg = R_()

    with es:
        kb.dma("sync", identb[:], cc_identb, wres=r_identb)
        kb.dma("sync", identf[:], cc_identf, wres=r_identf)
        kb.dma("sync", mask_sb[:], cc_mask, wres=r_mask)
        kb.dma("sync", onesb[:], cc_onesb, wres=r_onesb)
        kb.dma("sync", onesf[:], cc_onesf, wres=r_onesf)
        kb.dma("sync", P_CC, cc_ropec, wres=r_pcc)
        kb.dma("sync", P_SS, cc_ropes, wres=r_pss)
        kb.dma("sync", t_tmp[0:1, 0:1], temp_in, wres=r_ttmp)
        kb.op("vector", lambda e: e.reciprocal(out=t_tmp[0:1, 1:2], in_=t_tmp[0:1, 0:1]),
              reads=[r_ttmp], wres=r_ttmp)
        kb.op("vector", lambda e: e.tensor_scalar(out=t_tmp[0:1, 1:2],
                                                  in0=t_tmp[0:1, 1:2],
                                                  scalar1=-1.0, scalar2=None,
                                                  op0=ALU.mult),
              reads=[r_ttmp], wres=r_ttmp)
        kb.mmgroup([(ps_bc[:, 0:1], onesf[:], t_tmp[0:1, 1:2])],
                   r_psBC, reads=[r_onesf, r_ttmp])
        kb.op("scalar", lambda e: e.copy(out=nrt_sb[:], in_=ps_bc[:, 0:1]),
              reads=[r_psBC], wres=r_nrt)

        kb.dma("gpsimd", s_wkv.rearrange("a p d -> (a p) d"),
               wkv_pm.rearrange("a p d -> (a p) d"), wres=r_swkv, sem="q_ctw")
        kb.dma("sync", wkv_sb[:, 0:DC * 128], s_wkv[0], reads=[r_swkv],
               wres=r_wkv, sem="q_wkv")
        kb.dma("sync", wkv_sb[:, DC * 128:2 * DC * 128], s_wkv[1],
               reads=[r_swkv], wres=r_wkv, sem="q_wkv")
        for st in range(4 * num_tiles):
            sl = st % 2
            for half in range(2):
                xsl = half
                kb.dma("scalar", x_ring[:, ts(xsl, 512)],
                       x_in[ts(st, 128), ts(2 * half, 512)],
                       wres=r_xring[xsl], sem=f"q_xr{xsl}")
                kb.op("scalar", lambda e, sl=sl, half=half, xsl=xsl: e.copy(
                    out=ybf_ring[:, sl * D + 1024 * half: sl * D + 1024 * half + 512],
                    in_=x_ring[:, ts(xsl, 512)]),
                    reads=[r_xring[xsl]], wres=r_ybf[sl])
                kb.dma("scalar", x_ring[:, ts(xsl, 512)],
                       x_in[ts(st, 128), ts(2 * half + 1, 512)],
                       wres=r_xring[xsl], sem=f"q_xr{xsl}")
                kb.op("scalar", lambda e, sl=sl, half=half, xsl=xsl: e.copy(
                    out=ybf_ring[:, sl * D + 1024 * half + 512:
                                sl * D + 1024 * half + 1024],
                    in_=x_ring[:, ts(xsl, 512)]),
                    reads=[r_xring[xsl]], wres=r_ybf[sl])
            kb.dma("scalar", s_xbf[ts(st, 128), :], ybf_ring[:, ts(sl, D)],
                   reads=[r_ybf[sl]], wres=r_sxbf[st], sem="q_sxb")
            if st == 3:
                for dc in range(DC):
                    kb.dma("scalar" if dc % 2 == 0 else "sync",
                           xy_fm[:, ts(dc, TT)], s_xbf[0:TT, ts(dc, 128)],
                           reads=[r_sxbf[j] for j in range(4)],
                           wres=r_xyfm, tp=True, sem="q_xy")

        AROW = 12288
        ap_tmp = bass.AP(arena_f, 8192, [[AROW, 128], [1, 2048]])
        ap_te = bass.AP(arena_f, 8192, [[AROW, 128], [2, 1024]])
        ap_to = bass.AP(arena_f, 8193, [[AROW, 128], [2, 1024]])
        ap_se = bass.AP(arena_f, 4096, [[AROW, 128], [2, 1024]])
        ap_so = bass.AP(arena_f, 4097, [[AROW, 128], [2, 1024]])
        ap_ce = bass.AP(arena_f, 2048, [[AROW, 128], [2, 1024]])
        ap_co = bass.AP(arena_f, 2049, [[AROW, 128], [2, 1024]])
        for dc in range(DC):
            kb.dma("sync", P_STG, wqT[ts(dc, 128), :], wres=r_pstg,
                   sem="q_stg")
            kb.op("vector", lambda e: e.tensor_tensor(out=P_ROT, in0=P_STG,
                                                      in1=P_CC, op=ALU.mult),
                  reads=[r_pstg, r_pcc], wres=r_prot)
            kb.op("vector", lambda e: e.tensor_tensor(
                out=ap_te, in0=ap_so, in1=ap_ce, op=ALU.mult),
                reads=[r_pstg, r_pss], wres=r_ptmp)
            kb.op("vector", lambda e: e.tensor_tensor(
                out=ap_to, in0=ap_se, in1=ap_co, op=ALU.mult),
                reads=[r_pstg, r_pss], wres=r_ptmp)
            kb.op("vector", lambda e: e.tensor_tensor(
                out=arena_bf[:, 0:2048], in0=P_ROT, in1=ap_tmp, op=ALU.add),
                reads=[r_prot, r_ptmp], wres=r_wqstg)
            for half in range(2):
                    dst = bass.AP(s_wq.tensor,
                                  (half * 8) * (128 * DC * 128) + dc * 128,
                                  [[DC * 128, 128], [128 * DC * 128, 8], [1, 128]])
                    srcv = bass.AP(arena_bf, half * 8 * 128,
                                   [[32768, 128], [128, 8], [1, 128]])
                    kb.dma("sync" if half == 0 else "scalar", dst, srcv,
                           reads=[r_wqstg], wres=r_swq, sem="q_swq")

        for l in range(NL):
            for dc in range(DC):
                kb.dma("gpsimd", ga_sb[:, ts(l * DC + dc, R)],
                       ga_in[l, ts(dc, 128), :], wres=r_ga, sem="q_ctm")
                kb.dma("gpsimd", ua_sb[:, ts(l * DC + dc, R)],
                       ua_in[l, ts(dc, 128), :], wres=r_ua, sem="q_ctm")
                kb.dma("gpsimd", da_sb[:, ts(l * DC + dc, R)],
                       da_in[l, ts(dc, 128), :], wres=r_da, sem="q_ctm")
            kb.dma("gpsimd", gb_sb[64 * l:64 * l + 64, :], gb_in[l], wres=r_gb, sem="q_ctm")
            kb.dma("gpsimd", ub_sb[64 * l:64 * l + 64, :], ub_in[l], wres=r_ub, sem="q_ctm")
            kb.dma("gpsimd", db_sb[64 * l:64 * l + 64, :], db_in[l], wres=r_db, sem="q_ctm")
        for dc in range(DC):
            kb.dma("gpsimd", wr_sb[:, ts(dc, E)], wr_in[ts(dc, 128), :], wres=r_wr, sem="q_ctm")
        kb.dma("gpsimd", s_wo.rearrange("a p d -> (a p) d"),
               wo_blk.rearrange("a p d -> (a p) d"), wres=r_swo, sem="q_ctm")

        boundary_res = [r_pcc, r_pss, r_pstg, r_prot, r_ptmp, r_wqstg,
                        r_ybf[0], r_ybf[1], r_xring[0], r_xring[1], r_t1[0]]

        for t in range(num_tiles):
            tile_rows = ts(t, TT)

            if t == 0:
                seed(r_PT + [r_psb, r_qbf, r_S] + r_Sb + r_Srow[0]
                     + r_Srow[1] + r_Srow[2] + r_Prow + r_wq + r_att + r_acc
                     + r_res, boundary_res)
            else:
                seed(r_PT + [r_psb, r_qbf] + r_wq + r_att, r_wgu + r_wd + r_mid)
                seed([r_S] + r_Sb + r_Srow[0] + r_Srow[1] + r_Srow[2],
                     r_res + r_acc + r_res4)
                seed(r_Prow, r_mid)

            if t > 0:
                for dc in range(DC):
                    kb.dma("scalar" if dc % 2 == 0 else "sync",
                           xy_fm[:, ts(dc, TT)],
                           s_xbf[tile_rows, ts(dc, 128)],
                           reads=[r_sxbf[4 * t + j] for j in range(4)],
                           wres=r_xyfm, tp=True, sem="q_xy")

            kb.mmgroup([(ps_mmA[:], wkv_sb[:, ts(dc, 128)], xy_fm[:, ts(dc, TT)])
                        for dc in range(DC)], r_psA, reads=[r_wkv, r_xyfm])
            kb.op("scalar", lambda e, t=t: e.copy(out=k_fm[:, ts(t, TT)],
                                                  in_=ps_mmA[:]),
                  reads=[r_psA], wres=r_kfm)
            for qs in range(NQ):
                kb.mmgroup([(ps_mmB[:, 0:128],
                             xy_fm[:, dc * TT + qs * 128:dc * TT + (qs + 1) * 128],
                             wkv_sb[:, ts(DC + dc, 128)]) for dc in range(DC)],
                           r_psB, reads=[r_wkv, r_xyfm])
                kb.op("scalar", lambda e, t=t, qs=qs: e.copy(
                    out=v_tm[:, ts(4 * t + qs, 128)], in_=ps_mmB[:, 0:128]),
                    reads=[r_psB], wres=r_vtm)

            if dbg and t == 0:
                kb.dma("gpsimd", dbg_k, k_fm[:], reads=[r_kfm], sem="q_dbg")
                kb.dma("gpsimd", dbg_v, v_tm[:], reads=[r_vtm], sem="q_dbg")
                kb.dma("gpsimd", dbg_xfm, xy_fm[:], reads=[r_xyfm], sem="q_dbg")

            for h in range(H):
                if t == 0 and h % 2 == 0:
                    e_ = h // 2
                    gate = [("pe", kb.cnt["pe"])]
                    for smat, dmat, rres in ((s_wg, wg_blk, r_swg[e_]),
                                             (s_wu, wu_blk, r_swu[e_]),
                                             (s_wd, wd_blk, r_swd[e_])):
                        deps = list(gate)
                        fn = (lambda e, o=smat[e_].rearrange("a p d -> (a p) d"),
                              i=dmat[e_].rearrange("a p d -> (a p) d"):
                              e.dma_start(out=o, in_=i))
                        ev = kb.emit("gpsimd", fn, deps, inc=f"q_ct{e_}",
                                     inc_n=16)
                        rres.set_w(ev)
                wq_slot = h % 2
                kb.dma("sync", A_WQ(wq_slot), s_wq[h], reads=[r_swq],
                       wres=r_wq[wq_slot], sem=f"q_wq{wq_slot}")
                kb.mmgroup([(ps_mmA[:], A_WQ(wq_slot)[:, ts(dc, 128)],
                             xy_fm[:, ts(dc, TT)]) for dc in range(DC)],
                           r_psA, reads=[r_wq[wq_slot], r_xyfm])
                kb.op("scalar", lambda e: e.copy(out=A_QBF, in_=ps_mmA[:]),
                      reads=[r_psA], wres=r_qbf)
                def emit_S(qs):
                    qi = 4 * t + qs
                    W = (qi + 1) * 128
                    row = qs % 3
                    for kb_i in range(t + 1):
                        w0 = kb_i * 512
                        wN = min(512, W - w0)
                        bank, rbank = [(ps_mmB, r_psB), (ps_mmC, r_psC),
                                       (ps_mmA, r_psA)][kb_i % 3]
                        kb.mmgroup([(bank[:, 0:wN], A_QBF[:, ts(qs, 128)],
                                     k_fm[:, w0:w0 + wN])],
                                   rbank, reads=[r_qbf, r_kfm])
                        rs_ = r_Srow[row][kb_i]
                        if kb_i < t:
                            kb.op("vector", lambda e, w0=w0, wN=wN, bank=bank,
                                  row=row, qs=qs, kb_i=kb_i: e.tensor_scalar(
                                out=A_SROW(row)[:, w0:w0 + wN],
                                in0=bank[:, 0:wN], scalar1=0.0, scalar2=None,
                                op0=ALU.add, op1=ALU.max,
                                accum_out=smax_sb[:, qs * 8 + kb_i:
                                                  qs * 8 + kb_i + 1]),
                                reads=[rbank], wres=rs_, wres2=r_smax)
                        else:
                            if wN > 128:
                                kb.op("vector", lambda e, w0=w0, wN=wN, bank=bank,
                                      row=row: e.tensor_copy(
                                    out=A_SROW(row)[:, w0:w0 + wN - 128],
                                    in_=bank[:, 0:wN - 128]),
                                    reads=[rbank], wres=rs_)
                            kb.op("vector", lambda e, w0=w0, wN=wN, bank=bank,
                                  row=row: e.tensor_tensor(
                                out=A_SROW(row)[:, w0 + wN - 128:w0 + wN],
                                in0=bank[:, wN - 128:wN], in1=mask_sb[:],
                                op=ALU.add), reads=[rbank, r_mask], wres=rs_)
                            kb.op("vector", lambda e, w0=w0, wN=wN, row=row,
                                  qs=qs, kb_i=kb_i: e.reduce_max(
                                out=smax_sb[:, qs * 8 + kb_i:qs * 8 + kb_i + 1],
                                in_=A_SROW(row)[:, w0:w0 + wN], axis=AX.X),
                                reads=[rs_], wres=r_smax)
                    kb.op("vector", lambda e, qs=qs, nb=t + 1: e.reduce_max(
                        out=negmax_sb[:, qs:qs + 1],
                        in_=smax_sb[:, qs * 8:qs * 8 + nb], axis=AX.X,
                        negate=True), reads=[r_smax], wres=r_negmax)

                def emit_T(qs):
                    qi = 4 * t + qs
                    W = (qi + 1) * 128
                    row = qs % 3
                    prow = qs % 2
                    hs = (qs % 2) * 128
                    nblk = t + 1
                    for bb in range((nblk + 1) // 2):
                        w0 = bb * 1024
                        wN = min(1024, W - w0)
                        rds = [r_Srow[row][2 * bb], r_negmax]
                        if 2 * bb + 1 < nblk:
                            rds.append(r_Srow[row][2 * bb + 1])
                        kb.op("scalar", lambda e, w0=w0, wN=wN, qs=qs,
                              row=row, prow=prow:
                              e.activation(out=A_PROW[prow][:, w0:w0 + wN],
                                           in_=A_SROW(row)[:, w0:w0 + wN],
                                           func=AF.Exp,
                                           bias=negmax_sb[:, qs:qs + 1],
                                           scale=1.0, accum_out=sm2_sb[:, 1:2]),
                              reads=rds, wres=r_Prow[prow], wres2=r_sm2)
                        if bb == 0:
                            kb.op("vector", lambda e, qs=qs: e.tensor_copy(
                                out=rsum_sb[:, qs:qs + 1], in_=sm2_sb[:, 1:2]),
                                reads=[r_sm2], wres=r_rsum)
                        else:
                            kb.op("vector", lambda e, qs=qs: e.tensor_tensor(
                                out=rsum_sb[:, qs:qs + 1],
                                in0=rsum_sb[:, qs:qs + 1],
                                in1=sm2_sb[:, 1:2], op=ALU.add),
                                reads=[r_sm2, r_rsum], wres=r_rsum)
                        for half8 in range(2):
                            kc0 = bb * 8 + half8 * 4
                            if kc0 > qi:
                                break
                            nk4 = min(qi + 1, kc0 + 4) - kc0
                            tb, rtb = ((ps_tb1, r_psT1) if half8 == 0
                                       else (ps_tb2, r_psT2))
                            for j in range(nk4):
                                kb.tmm(tb[:, ts(j, 128)],
                                       A_PROW[prow][:, ts(kc0 + j, 128)],
                                       identb[:], rtb, reads=[r_Prow[prow]],
                                       ident_res=r_identb)
                            pt_dst = bass.AP(arena_bf, kc0 * 256 + hs,
                                             [[32768, 128], [256, nk4], [1, 128]])
                            wl = [r_PT[kc0 + j] for j in range(nk4)]
                            if half8 == 0:
                                kb.op("scalar", lambda e, d=pt_dst, tb=tb,
                                      nk4=nk4: e.copy(out=d,
                                                      in_=tb[:, 0:nk4 * 128]),
                                      reads=[rtb], wres=wl)
                            else:
                                kb.op("vector", lambda e, d=pt_dst, tb=tb,
                                      nk4=nk4: e.tensor_copy(
                                    out=d, in_=tb[:, 0:nk4 * 128]),
                                    reads=[rtb], wres=wl)
                    if qs % 2 == 1:
                        kb.op("vector", lambda e, qi=qi: e.memset(
                            A_PT(qi)[:, 0:128], 0.0), wres=r_PT[qi])

                def emit_PV(Hh):
                    qs_hi = 2 * Hh + 1
                    nk = 4 * t + qs_hi + 1
                    kb.mmgroup([(ps_pv[:, 0:256], v_tm[:, ts(kc, 128)],
                                 A_PT(kc)[:, 0:256]) for kc in range(nk)],
                               r_psPV,
                               reads=[r_vtm] + [r_PT[kc] for kc in range(nk)])

                def emit_norm_a(Hh):
                    for j in range(2):
                        qsj = 2 * Hh + j
                        kb.transpose(ps_tf[0:1, ts(j, 128)],
                                     rsum_sb[:, qsj:qsj + 1], identf[:],
                                     r_psTF, reads=[r_rsum], ident_res=r_identf)
                    kb.op("scalar", lambda e: e.copy(out=srow_sb[0:1, 0:256],
                                                     in_=ps_tf[0:1, 0:256]),
                          reads=[r_psTF], wres=r_srow)
                    kb.op("vector", lambda e: e.reciprocal(
                        out=rrow_sb[0:1, 0:256], in_=srow_sb[0:1, 0:256]),
                        reads=[r_srow], wres=r_rrow)

                def emit_norm_b(Hh, h):
                    kb.mmgroup([(ps_bc[:, 0:256], onesb[:], rrow_sb[0:1, 0:256])],
                               r_psBC, reads=[r_onesb, r_rrow])
                    kb.op("scalar", lambda e: e.copy(out=bcast_sb[:],
                                                     in_=ps_bc[:, 0:256]),
                          reads=[r_psBC], wres=r_bcast)
                    kb.op("vector", lambda e, h=h, Hh=Hh: e.tensor_tensor(
                        out=A_ATT(h)[:, ts(Hh, 256)], in0=ps_pv[:, 0:256],
                        in1=bcast_sb[:], op=ALU.mult),
                        reads=[r_psPV, r_bcast], wres=r_att[h])

                emit_S(0)
                emit_S(1)
                emit_S(2)
                emit_T(0)
                emit_S(3)
                emit_T(1)
                emit_PV(0)
                emit_norm_a(0)
                emit_T(2)
                emit_norm_b(0, h)
                emit_T(3)
                emit_norm_a(1)
                emit_PV(1)
                emit_norm_b(1, h)

            if dbg and t == 0:
                for h in range(H):
                    kb.dma("gpsimd", dbg_att[h], A_ATT(h), reads=[r_att[h]], sem="q_dbg")

            seed(r_wo, r_PT + [r_psb] + r_wq + [r_qbf])
            seed(r_res4, r_acc + r_Srow[0] + r_Srow[1])
            seed(r_res, r_Srow[2])
            seed(r_res, [r_S])

            def ln_pass(res_ap, res_res, out_dst, out_wres):
                kb.op("vector", lambda e, res_ap=res_ap: e.bn_stats(
                    out=bnt_sb[:, 0:6], in_=res_ap[:, 0:512]),
                    reads=[res_res], wres=r_bnt)
                for kq in range(1, 4):
                    kb.op("vector", lambda e, kq=kq, res_ap=res_ap: e.bn_stats(
                        out=bnt_sb[:, ts(kq, 6)], in_=res_ap[:, ts(kq, 512)]),
                        reads=[res_res], wres=r_bnt)
                kb.op("vector", lambda e: e.bn_aggr(out=mv_sb[:], in_=bnt_sb[:]),
                      reads=[r_bnt], wres=r_mv)
                kb.op("vector", lambda e: e.tensor_scalar(
                    out=ln_sc[:, 0:1], in0=mv_sb[:, 1:2], scalar1=EPS, scalar2=None,
                    op0=ALU.add), reads=[r_mv], wres=r_lnsc)
                kb.op("scalar", lambda e: e.activation(
                    out=ln_sc[:, 3:4], in_=ln_sc[:, 0:1], func=AF.Ln),
                    reads=[r_lnsc], wres=r_lnsc)
                kb.op("vector", lambda e: e.tensor_scalar(
                    out=ln_sc[:, 3:4], in0=ln_sc[:, 3:4], scalar1=-0.5, scalar2=None,
                    op0=ALU.mult), reads=[r_lnsc], wres=r_lnsc)
                kb.op("scalar", lambda e: e.activation(
                    out=ln_sc[:, 1:2], in_=ln_sc[:, 3:4], func=AF.Exp),
                    reads=[r_lnsc], wres=r_lnsc)
                kb.op("vector", lambda e: e.scalar_tensor_tensor(
                    out=ln_sc[:, 2:3], in0=mv_sb[:, 0:1], scalar=-1.0,
                    in1=ln_sc[:, 1:2], op0=ALU.mult, op1=ALU.mult),
                    reads=[r_mv, r_lnsc], wres=r_lnsc)
                kb.op("scalar", lambda e, res_ap=res_ap, out_dst=out_dst:
                      e.activation(out=out_dst, in_=res_ap[:],
                                   func=AF.Identity, scale=ln_sc[:, 1:2],
                                   bias=ln_sc[:, 2:3]),
                      reads=[res_res, r_lnsc], wres=out_wres)

            for db8 in range(8):
                wo_slot = db8 % 4
                kb.dma("sync", A_WO(wo_slot), s_wo[db8], reads=[r_swo],
                       wres=r_wo[wo_slot], sem=f"q_wo{wo_slot}")
                for qs in range(NQ):
                    bank, rbank = ((ps_mmA, r_psA) if qs % 2 == 0
                                   else (ps_mmB, r_psB))
                    kb.mmgroup([(bank[:, 0:256], A_ATT(c)[:, ts(qs, 128)],
                                 A_WO(wo_slot)[:, ts(c, 256)]) for c in range(DC)],
                               rbank,
                               reads=[r_wo[wo_slot]] + [r_att[c] for c in range(DC)])
                    kb.op("vector", lambda e, db8=db8, bank=bank, qs=qs:
                          e.tensor_copy(out=A_RES4(qs)[:, ts(db8, 256)],
                                        in_=bank[:, 0:256]),
                          reads=[rbank], wres=r_res4[qs])
            for qs in range(NQ):
                xs = qs % 2
                kb.dma("scalar", A_RES(xs)[:], x_in[ts(4 * t + qs, 128), :],
                       reads=[], wres=r_res[xs], sem=f"q_xr{xs}")
                kb.op("vector", lambda e, qs=qs, xs=xs: e.tensor_tensor(
                    out=A_RES4(qs)[:], in0=A_RES4(qs), in1=A_RES(xs)[:],
                    op=ALU.add), reads=[r_res[xs], r_res4[qs]], wres=r_res4[qs])
                if dbg and t == 0 and qs == 0:
                    kb.dma("gpsimd", dbg_res, A_RES4(qs)[:],
                           reads=[r_res4[qs]], sem="q_dbg")
                ydst = ybf_ring[:, ts(qs % 2, D)]
                ln_pass(A_RES4(qs), r_res4[qs], ydst, r_ybf[qs % 2])
                kb.dma("gpsimd", s_y[ts(4 * t + qs, 128), :], ydst,
                       reads=[r_ybf[qs % 2]], wres=r_sy[4 * t + qs],
                       sem="q_sy")
                for g4 in range(4):
                    tb, rtb = ((ps_tb1, r_psT1) if g4 % 2 == 0
                               else (ps_tb2, r_psT2))
                    for j in range(4):
                        dc = g4 * 4 + j
                        kb.tmm(tb[:, ts(j, 128)],
                               ybf_ring[:, (qs % 2) * D + dc * 128:
                                        (qs % 2) * D + (dc + 1) * 128],
                               identb[:], rtb, reads=[r_ybf[qs % 2]],
                               ident_res=r_identb)
                    y_dst = bass.AP(xy_fm, (g4 * 4) * TT + qs * 128,
                                    [[DC * TT, 128], [TT, 4], [1, 128]])
                    if g4 % 2 == 0:
                        kb.op("scalar", lambda e, d=y_dst, tb=tb:
                              e.copy(out=d, in_=tb[:, 0:512]),
                              reads=[rtb], wres=r_xyfm)
                    else:
                        kb.op("vector", lambda e, d=y_dst, tb=tb:
                              e.tensor_copy(out=d, in_=tb[:, 0:512]),
                              reads=[rtb], wres=r_xyfm)


            for qs in range(NQ):
                kb.mmgroup([(ps_pv[:, 0:8],
                             xy_fm[:, dc * TT + qs * 128:dc * TT + (qs + 1) * 128],
                             wr_sb[:, ts(dc, 8)]) for dc in range(DC)],
                           r_psPV, reads=[r_xyfm, r_wr])
                RT = lambda j: rt_sb[:, ts(j, 8)]
                S1 = lambda j: rt1_sb[:, j:j + 1]
                kb.op("vector", lambda e: e.tensor_copy(out=RT(0), in_=ps_pv[:, 0:8]),
                      reads=[r_psPV], wres=r_rt)
                kb.op("scalar", lambda e: e.activation(out=RT(1), in_=RT(0),
                                                       func=AF.Exp,
                                                       scale=nrt_sb[:, 0:1]),
                      reads=[r_rt, r_nrt], wres=r_rt)
                kb.op("vector", lambda e: e.tensor_scalar(
                    out=RT(1), in0=RT(1), scalar1=1.0, scalar2=None, op0=ALU.add),
                    reads=[r_rt], wres=r_rt)
                kb.op("vector", lambda e: e.reciprocal(out=RT(2), in_=RT(1)),
                      reads=[r_rt], wres=r_rt)
                kb.op("vector", lambda e: e.tensor_scalar(
                    out=RT(3), in0=RT(2), scalar1=GOLDEN_LOWER, scalar2=None,
                    op0=ALU.is_ge), reads=[r_rt], wres=r_rt)
                kb.op("vector", lambda e: e.tensor_scalar(
                    out=RT(4), in0=RT(2), scalar1=GOLDEN_UPPER, scalar2=None,
                    op0=ALU.is_le), reads=[r_rt], wres=r_rt)
                kb.op("vector", lambda e: e.tensor_tensor(
                    out=RT(3), in0=RT(3), in1=RT(4), op=ALU.mult),
                    reads=[r_rt], wres=r_rt)
                kb.op("vector", lambda e: e.tensor_scalar(
                    out=RT(5), in0=RT(2), scalar1=GOLDEN_CENTER, scalar2=None,
                    op0=ALU.subtract), reads=[r_rt], wres=r_rt)
                kb.op("scalar", lambda e: e.activation(
                    out=RT(5), in_=RT(5), func=AF.Abs), reads=[r_rt], wres=r_rt)
                kb.op("vector", lambda e: e.tensor_scalar(
                    out=RT(6), in0=RT(5), scalar1=-10.0, scalar2=None, op0=ALU.mult),
                    reads=[r_rt], wres=r_rt)
                kb.op("scalar", lambda e: e.activation(out=RT(6), in_=RT(6),
                                                       func=AF.Exp),
                      reads=[r_rt], wres=r_rt)
                kb.op("vector", lambda e: e.tensor_tensor(
                    out=RT(6), in0=RT(6), in1=RT(3), op=ALU.mult),
                    reads=[r_rt], wres=r_rt)
                kb.op("vector", lambda e: e.reduce_sum(out=S1(0), in_=RT(6),
                                                       axis=AX.X),
                      reads=[r_rt], wres=r_rt1)
                kb.op("vector", lambda e: e.tensor_scalar(
                    out=S1(1), in0=S1(0), scalar1=1e-8, scalar2=None, op0=ALU.is_lt),
                    reads=[r_rt1], wres=r_rt1)
                kb.op("vector", lambda e: e.tensor_scalar(
                    out=S1(2), in0=S1(1), scalar1=-1.0, scalar2=1.0,
                    op0=ALU.mult, op1=ALU.add), reads=[r_rt1], wres=r_rt1)
                kb.op("vector", lambda e: e.tensor_scalar(
                    out=RT(7), in0=RT(5), scalar1=-1.0 / 0.3, scalar2=None,
                    op0=ALU.mult), reads=[r_rt], wres=r_rt)
                kb.op("scalar", lambda e: e.activation(out=RT(7), in_=RT(7),
                                                       func=AF.Exp),
                      reads=[r_rt], wres=r_rt)
                kb.op("vector", lambda e: e.reduce_max(out=S1(3), in_=RT(7),
                                                       axis=AX.X),
                      reads=[r_rt], wres=r_rt1)
                kb.op("vector", lambda e: e.tensor_scalar(
                    out=RT(8), in0=RT(7), scalar1=S1(3), scalar2=None, op0=ALU.is_ge),
                    reads=[r_rt, r_rt1], wres=r_rt)
                kb.op("vector", lambda e: e.tensor_scalar(
                    out=RT(9), in0=RT(8), scalar1=-1.0, scalar2=1.0,
                    op0=ALU.mult, op1=ALU.add), reads=[r_rt], wres=r_rt)
                kb.op("vector", lambda e: e.tensor_tensor(
                    out=RT(9), in0=RT(7), in1=RT(9), op=ALU.mult),
                    reads=[r_rt], wres=r_rt)
                kb.op("vector", lambda e: e.reduce_max(out=S1(4), in_=RT(9),
                                                       axis=AX.X),
                      reads=[r_rt], wres=r_rt1)
                kb.op("vector", lambda e: e.tensor_scalar(
                    out=RT(10), in0=RT(9), scalar1=S1(4), scalar2=None, op0=ALU.is_ge),
                    reads=[r_rt, r_rt1], wres=r_rt)
                kb.op("vector", lambda e: e.tensor_tensor(
                    out=RT(10), in0=RT(10), in1=RT(8), op=ALU.add),
                    reads=[r_rt], wres=r_rt)
                kb.op("vector", lambda e: e.tensor_tensor(
                    out=RT(10), in0=RT(10), in1=RT(7), op=ALU.mult),
                    reads=[r_rt], wres=r_rt)
                kb.op("vector", lambda e: e.reduce_sum(out=S1(5), in_=RT(10),
                                                       axis=AX.X),
                      reads=[r_rt], wres=r_rt1)
                kb.op("vector", lambda e: e.tensor_scalar(
                    out=S1(5), in0=S1(5), scalar1=1e-8, scalar2=None, op0=ALU.max),
                    reads=[r_rt1], wres=r_rt1)
                kb.op("vector", lambda e: e.reciprocal(out=S1(6), in_=S1(5)),
                      reads=[r_rt1], wres=r_rt1)
                kb.op("vector", lambda e: e.tensor_scalar(
                    out=RT(10), in0=RT(10), scalar1=S1(6), scalar2=S1(1),
                    op0=ALU.mult, op1=ALU.mult), reads=[r_rt, r_rt1], wres=r_rt)
                kb.op("vector", lambda e: e.tensor_scalar(
                    out=RT(11), in0=RT(6), scalar1=S1(2), scalar2=None, op0=ALU.mult),
                    reads=[r_rt, r_rt1], wres=r_rt)
                kb.op("vector", lambda e: e.tensor_tensor(
                    out=RT(11), in0=RT(11), in1=RT(10), op=ALU.add),
                    reads=[r_rt], wres=r_rt)
                kb.op("vector", lambda e: e.reduce_sum(out=S1(7), in_=RT(11),
                                                       axis=AX.X),
                      reads=[r_rt], wres=r_rt1)
                kb.op("vector", lambda e: e.tensor_scalar(
                    out=S1(7), in0=S1(7), scalar1=1e-8, scalar2=None, op0=ALU.max),
                    reads=[r_rt1], wres=r_rt1)
                kb.op("vector", lambda e: e.reciprocal(out=S1(0), in_=S1(7)),
                      reads=[r_rt1], wres=r_rt1)
                kb.op("vector", lambda e, qs=qs: e.tensor_scalar(
                    out=w_tm[:, ts(qs, E)], in0=RT(11), scalar1=S1(0), scalar2=None,
                    op0=ALU.mult), reads=[r_rt, r_rt1], wres=r_wtm)

            seed(r_wgu + r_wd + r_mid,
                 r_wo + r_att + [r_qbf, r_psb] + r_PT + r_wq + r_Prow)
            seed(r_acc, r_res4)
            for e_ in range(E):
                for qs in range(NQ):
                    kb.transpose(ps_tf[0:1, 256:384],
                                 w_tm[:, qs * E + e_:qs * E + e_ + 1], identf[:],
                                 r_psTF, reads=[r_wtm], ident_res=r_identf)
                    kb.op("scalar", lambda e, qs=qs: e.copy(
                        out=rrow_sb[0:1, ts(qs, 128)], in_=ps_tf[0:1, 256:384]),
                        reads=[r_psTF], wres=r_rrow)
                kb.mmgroup([(ps_bc[:], onesb[:], rrow_sb[0:1, 0:512])],
                           r_psBC, reads=[r_onesb, r_rrow])
                kb.op("scalar", lambda e: e.copy(out=wb_sb[:], in_=ps_bc[:]),
                      reads=[r_psBC], wres=r_wb)

                if e_ < NL:
                    l = e_
                    for nm in ("g", "u"):
                        asrc = ga_sb if nm == "g" else ua_sb
                        ares_src = r_ga if nm == "g" else r_ua
                        adst = a_fm if nm == "g" else au_fm
                        ares = r_afm if nm == "g" else r_aufm
                        for qs in range(NQ):
                            kb.mmgroup([(ps_pv[:, 0:64],
                                         xy_fm[:, dc * TT + qs * 128:
                                               dc * TT + (qs + 1) * 128],
                                         asrc[:, ts(l * DC + dc, R)])
                                        for dc in range(DC)], r_psPV,
                                       reads=[r_xyfm, ares_src])
                            kb.op("scalar", lambda e: e.copy(out=aev_sb[:, 0:64],
                                                             in_=ps_pv[:, 0:64]),
                                  reads=[r_psPV], wres=r_aev)
                            kb.tmm(ps_tb1[0:64, 0:128], aev_sb[:, 0:64],
                                   identb[:], r_psT1, reads=[r_aev],
                                   ident_res=r_identb)
                            kb.op("scalar", lambda e, qs=qs, adst=adst: e.copy(
                                out=adst[0:64, ts(qs, 128)], in_=ps_tb1[0:64, 0:128]),
                                reads=[r_psT1], wres=ares)
                            kb.op("scalar", lambda e, qs=qs, adst=adst: e.copy(
                                out=adst[64:128, ts(qs, 128)], in_=ps_tb1[0:64, 0:128]),
                                reads=[r_psT1], wres=ares)

                for fb in range(8):
                    gu_g = fb % 2
                    gu_u = 2 + fb % 2
                    kb.dma("sync", A_WGU(gu_g), s_wg[e_, fb], reads=[r_swg[e_]],
                           wres=r_wgu[gu_g], sem=f"q_gu{gu_g}")
                    kb.dma("sync", A_WGU(gu_u), s_wu[e_, fb], reads=[r_swu[e_]],
                           wres=r_wgu[gu_u], sem=f"q_gu{gu_u}")
                    for fs in range(2):
                        fc = 2 * fb + fs
                        gbank, rgbank = ((ps_mmA, r_psA) if fc % 2 == 0
                                         else (ps_tf, r_psTF))
                        mms = [(gbank[:], A_WGU(gu_g)[:, dc * 256 + fs * 128:
                                                      dc * 256 + fs * 128 + 128],
                                xy_fm[:, ts(dc, TT)]) for dc in range(DC)]
                        rds = [r_wgu[gu_g], r_xyfm]
                        if e_ < NL:
                            mms.append((gbank[:],
                                        gb_sb[64 * e_:64 * e_ + 64,
                                              fc * 128:(fc + 1) * 128],
                                        a_fm[64 * e_:64 * e_ + 64, :]))
                            rds += [r_gb, r_afm]
                        kb.mmgroup(mms, rgbank, reads=rds)
                        sgs = fc % 2
                        kb.op("scalar", lambda e, sgs=sgs, gbank=gbank:
                              e.activation(out=sg_ring[:, ts(sgs, 512)],
                                           in_=gbank[:], func=AF.Silu),
                              reads=[rgbank], wres=r_sg[sgs])
                        ubank, rubank = ((ps_mmB, r_psB) if fc % 2 == 0
                                         else (ps_bc, r_psBC))
                        mms = [(ubank[:], A_WGU(gu_u)[:, dc * 256 + fs * 128:
                                                      dc * 256 + fs * 128 + 128],
                                xy_fm[:, ts(dc, TT)]) for dc in range(DC)]
                        rds = [r_wgu[gu_u], r_xyfm]
                        if e_ < NL:
                            mms.append((ubank[:],
                                        ub_sb[64 * e_:64 * e_ + 64,
                                              fc * 128:(fc + 1) * 128],
                                        au_fm[64 * e_:64 * e_ + 64, :]))
                            rds += [r_ub, r_aufm]
                        kb.mmgroup(mms, rubank, reads=rds)
                        kb.op("vector", lambda e, fc=fc, sgs=sgs, ubank=ubank:
                              e.tensor_tensor(out=A_MID(fc), in0=ubank[:],
                                              in1=sg_ring[:, ts(sgs, 512)],
                                              op=ALU.mult),
                              reads=[rubank, r_sg[sgs]], wres=r_mid[fc])

                if e_ < NL:
                    l = e_
                    kb.mmgroup([(ps_pv[0:64, :],
                                 da_sb[:, ts(l * FC + fc, R)], A_MID(fc))
                                for fc in range(FC)], r_psPV,
                               reads=[r_da] + [r_mid[fc] for fc in range(FC)])
                    kb.op("scalar", lambda e: e.copy(out=am_fm[0:64, :],
                                                     in_=ps_pv[0:64, :]),
                          reads=[r_psPV], wres=r_amfm)
                    kb.op("scalar", lambda e: e.copy(out=am_fm[64:128, :],
                                                     in_=ps_pv[0:64, :]),
                          reads=[r_psPV], wres=r_amfm)

                for db2 in range(8):
                    wd_slot = db2 % 2
                    kb.dma("sync", A_WD(wd_slot), s_wd[e_, db2], reads=[r_swd[e_]],
                           wres=r_wd[wd_slot], sem=f"q_wd{wd_slot}")
                    for ds in range(2):
                        dchunk = 2 * db2 + ds
                        dbank, rdbank = ((ps_mmC, r_psC) if dchunk % 2 == 0
                                         else (ps_pv, r_psPV))
                        mms = [(dbank[:],
                                A_WD(wd_slot)[:, fc * 256 + ds * 128:
                                              fc * 256 + ds * 128 + 128],
                                A_MID(fc)) for fc in range(FC)]
                        rds = [r_wd[wd_slot]] + [r_mid[fc] for fc in range(FC)]
                        if e_ < NL:
                            mms.append((dbank[:],
                                        db_sb[64 * e_:64 * e_ + 64,
                                              dchunk * 128:(dchunk + 1) * 128],
                                        am_fm[64 * e_:64 * e_ + 64, :]))
                            rds += [r_db, r_amfm]
                        kb.mmgroup(mms, rdbank, reads=rds)
                        if e_ == 0:
                            kb.op("vector", lambda e, dchunk=dchunk, dbank=dbank:
                                  e.tensor_tensor(out=A_ACC(dchunk), in0=dbank[:],
                                                  in1=wb_sb[:], op=ALU.mult),
                                  reads=[rdbank, r_wb], wres=r_acc[dchunk])
                        else:
                            t1s = dchunk % 2
                            kb.op("vector", lambda e, dbank=dbank, t1s=t1s:
                                  e.tensor_tensor(out=t1_ring[:, ts(t1s, 512)],
                                                  in0=dbank[:], in1=wb_sb[:],
                                                  op=ALU.mult),
                                  reads=[rdbank, r_wb], wres=r_t1[t1s])
                            kb.op("vector", lambda e, dchunk=dchunk, t1s=t1s:
                                  e.tensor_tensor(out=A_ACC(dchunk),
                                                  in0=A_ACC(dchunk),
                                                  in1=t1_ring[:, ts(t1s, 512)],
                                                  op=ALU.add),
                                  reads=[r_t1[t1s], r_acc[dchunk]],
                                  wres=r_acc[dchunk])

            for qs in range(NQ):
                y2s = qs % 2
                kb.dma("scalar", ybf_ring[:, ts(y2s, D)], s_y[ts(4 * t + qs, 128), :],
                       reads=[r_sy[4 * t + qs]], wres=r_ybf[y2s],
                       sem=f"q_yb{y2s}")
                res_slot = qs % 2
                for dc in range(DC):
                    kb.transpose(ps_tf[:, ts(dc % 4, 128)],
                                 A_ACC(dc)[:, ts(qs, 128)], identf[:],
                                 r_psTF, reads=[r_acc[dc]], ident_res=r_identf)
                    kb.op("vector", lambda e, dc=dc, y2s=y2s, res_slot=res_slot:
                          e.tensor_tensor(
                        out=A_RES(res_slot)[:, ts(dc, 128)],
                        in0=ps_tf[:, ts(dc % 4, 128)],
                        in1=ybf_ring[:, y2s * D + dc * 128:y2s * D + (dc + 1) * 128],
                        op=ALU.add),
                        reads=[r_psTF, r_ybf[y2s]], wres=r_res[res_slot])
                out_slot = (qs + 1) % 2
                ln_pass(A_RES(res_slot), r_res[res_slot], A_RES(out_slot)[:],
                        r_res[out_slot])
                kb.dma("gpsimd", out_ap[ts(4 * t + qs, 128), :],
                       A_RES(out_slot)[:], reads=[r_res[out_slot]], wres=r_out,
                       sem="q_out")

        kb.emit("gpsimd", lambda e: None,
                [(s, kb.cnt[s]) for s in DMA_SEMS if kb.cnt[s] > 0])

        block = es.enter_context(nc.Block())

        def runner(eng):
            def run(engobj):
                with nc.allow_low_precision(
                        reason="bf16 matmul operands with fp32 accumulation are "
                               "the intended precision policy for this kernel"):
                    for fn, waits, inc, inc_n in kb.ops[eng]:
                        for s, v in waits:
                            engobj.wait_ge(sems[s], v)
                        ins = fn(engobj)
                        if inc is not None and ins is not None:
                            ins.then_inc(sems[inc], inc_n)
            return run

        block.tensor(runner("tensor"))
        block.vector(runner("vector"))
        block.scalar(runner("scalar"))
        block.sync(runner("sync"))
        block.gpsimd(runner("gpsimd"))

    return nc


def _consts():
    ident = np.eye(128, dtype=np.float32)
    mask = np.where(np.tril(np.ones((128, 128), dtype=bool)), 0.0,
                    -1e9).astype(np.float32)
    theta = 1.0 / 10000.0 ** (np.arange(0, 32, dtype=np.float64) / 32.0)
    cc = np.zeros((D,), dtype=np.float64)
    ss = np.zeros((D,), dtype=np.float64)
    for h in range(H):
        for p in range(64):
            ang = h * theta[p % 32]
            cc[128 * h + 2 * p] = math.cos(ang)
            cc[128 * h + 2 * p + 1] = math.cos(ang)
            ss[128 * h + 2 * p] = -math.sin(ang)
            ss[128 * h + 2 * p + 1] = math.sin(ang)
    cc *= SCALING
    ss *= SCALING
    return {
        "cc_identb": ident.astype(ml_dtypes.bfloat16),
        "cc_identf": ident,
        "cc_mask": mask,
        "cc_ropec": np.broadcast_to(cc.astype(np.float32), (128, D)).copy(),
        "cc_ropes": np.broadcast_to(ss.astype(np.float32), (128, D)).copy(),
        "cc_onesb": np.ones((1, 128), dtype=np.float32).astype(ml_dtypes.bfloat16),
        "cc_onesf": np.ones((1, 128), dtype=np.float32),
    }


def _prep_weights(i):
    c = np.ascontiguousarray
    f32 = lambda a: np.asarray(a, dtype=np.float32)
    wq = f32(i["L0_attn_Wq"])
    wk = f32(i["L0_attn_Wk"])
    wv = f32(i["L0_attn_Wv"])
    wo = f32(i["L0_attn_Wo"])
    wg = f32(i["L0_ffn_Wg"])
    wu = f32(i["L0_ffn_Wu"])
    wd = f32(i["L0_ffn_Wd"])
    wkv_pm = np.stack([
        wk.T.reshape(16, 128, 128).transpose(1, 0, 2).reshape(128, DC * 128),
        wv.T.reshape(16, 128, 128).transpose(1, 0, 2).reshape(128, DC * 128)])
    wo_b = wo.T.reshape(16, 128, 8, 256).transpose(2, 1, 0, 3).reshape(8, 128, DC * 256)
    blk = lambda w: (w.reshape(8, 16, 128, 8, 256).transpose(0, 3, 2, 1, 4)
                     .reshape(8, 8, 128, 16 * 256))
    return {
        "wqT": c(wq.T),
        "wkv_pm": c(wkv_pm),
        "wo_blk": c(wo_b),
        "wg_blk": c(blk(wg)),
        "wu_blk": c(blk(wu)),
        "wd_blk": c(blk(wd)),
        "ga": f32(i["L0_ffn_Ga"]),
        "gb": f32(i["L0_ffn_Gb"]),
        "ua": f32(i["L0_ffn_Ua"]),
        "ub": f32(i["L0_ffn_Ub"]),
        "da": f32(i["L0_ffn_Da"]),
        "db": f32(i["L0_ffn_Db"]),
        "wr": f32(i["L0_ffn_Wr"]),
        "temp": f32(i["L0_ffn_temperature"]).reshape(1, 1),
    }


def kernel(**inputs):
    x = np.asarray(inputs["L0_attn_hidden_states"], dtype=np.float32)
    wm = _prep_weights(inputs)
    wm.update(_consts())
    nc = build_nc()
    in_maps = []
    for b in range(B):
        m = dict(wm)
        m["x"] = np.ascontiguousarray(x[b])
        in_maps.append(m)
    res = run_bass_kernel_spmd(nc, in_maps, core_ids=list(range(B)))
    return np.stack([r["out"] for r in res.results], axis=0)


# version 12
# speedup vs baseline: 1.0568x; 1.0191x over previous
import math
from contextlib import ExitStack

import numpy as np
import ml_dtypes

import concourse.bass as bass
import concourse.mybir as mybir
from concourse.bass_utils import run_bass_kernel_spmd

F32 = mybir.dt.float32
BF16 = mybir.dt.bfloat16
AF = mybir.ActivationFunctionType
ALU = mybir.AluOpType
AX = mybir.AxisListType

B, T, D = 8, 4096, 2048
H, DH = 16, 128
E, F, R, NL = 8, 2048, 64, 2
NT = 8
TT = 512
NQ = 4
DC = 16
FC = 16
EPS = 1e-5
GOLDEN_CENTER = 1.0 / math.e
GOLDEN_LOWER = 0.5 - math.log(4.0 / 3.0)
GOLDEN_UPPER = 0.5
SCALING = DH ** (-0.5)

ENGINES = ("tensor", "vector", "scalar", "sync", "gpsimd")
ESEM = {"tensor": "pe", "vector": "dve", "scalar": "act"}
DMA_SEMS = (["q_cn", "q_stg", "q_wkv", "q_ctw", "q_swq", "q_xy", "q_ctm", "q_sxb",
             "q_sy", "q_out", "q_dbg"]
            + [f"q_wq{i}" for i in range(2)] + [f"q_wo{i}" for i in range(4)]
            + [f"q_gu{i}" for i in range(4)] + [f"q_wd{i}" for i in range(2)]
            + [f"q_xr{i}" for i in range(2)] + [f"q_yb{i}" for i in range(2)]
            + [f"q_ct{i}" for i in range(8)])
ALL_SEMS = ["pe", "dve", "act"] + DMA_SEMS


class Res:
    __slots__ = ("w", "rs")

    def __init__(self):
        self.w = None
        self.rs = []

    def for_write(self):
        d = list(self.rs)
        if self.w is not None:
            d.append(self.w)
        return d

    def for_read(self):
        return [self.w] if self.w is not None else []

    def all_evs(self):
        return self.for_write()

    def set_w(self, ev):
        self.w = ev
        self.rs = []

    def add_r(self, ev):
        self.rs.append(ev)


def seed(new_reses, old_reses):
    evs = []
    for o in old_reses:
        evs += o.all_evs()
    for n in new_reses:
        n.rs.extend(evs)


class KB:
    def __init__(self, sems):
        self.sems = sems
        self.ops = {e: [] for e in ENGINES}
        self.cnt = {s: 0 for s in ALL_SEMS}
        self.lastw = {}

    def emit(self, eng, fn, deps=(), inc=None, inc_n=1):
        seen = {}
        for d in deps:
            if d is None:
                continue
            s, v = d
            if v <= 0:
                continue
            if s.startswith("q_"):
                v = self.cnt[s]
                if v <= 0:
                    continue
            if seen.get(s, 0) < v:
                seen[s] = v
        waits = []
        for s, v in seen.items():
            if eng == "tensor" and s == "pe":
                continue
            if self.lastw.get((eng, s), 0) >= v:
                continue
            self.lastw[(eng, s)] = v
            waits.append((s, v))
        ev = None
        if inc is not None:
            self.cnt[inc] += inc_n
            ev = (inc, self.cnt[inc])
        self.ops[eng].append((fn, waits, inc, inc_n))
        return ev

    def dma(self, q, out, in_, reads=(), wres=None, tp=False, sem="q_cn"):
        deps = list(wres.for_write()) if wres is not None else []
        for r in reads:
            deps += r.for_read()
        if tp:
            fn = lambda e, o=out, i=in_: e.dma_start_transpose(out=o, in_=i)
        else:
            fn = lambda e, o=out, i=in_: e.dma_start(out=o, in_=i)
        ev = self.emit(q, fn, deps, inc=sem, inc_n=16)
        if wres is not None:
            wres.set_w(ev)
        for r in reads:
            r.add_r(ev)
        return ev

    def mmgroup(self, mms, bank, reads=()):
        deps = list(bank.for_write())
        for r in reads:
            deps += r.for_read()
        n = len(mms)
        ev = None
        for i, (o, l, r_) in enumerate(mms):
            st, sp = (i == 0), (i == n - 1)
            fn = (lambda e, o=o, l=l, r_=r_, st=st, sp=sp:
                  e.matmul(o, l, r_, start=st, stop=sp))
            ev = self.emit("tensor", fn, deps if i == 0 else (),
                           inc="pe" if sp else None)
        bank.set_w(ev)
        for r in reads:
            r.add_r(ev)
        return ev

    def tmm(self, out, in_, ident, bank, reads=(), ident_res=None):
        deps = list(bank.for_write())
        for r in reads:
            deps += r.for_read()
        if ident_res is not None:
            deps += ident_res.for_read()
        fn = lambda e, o=out, i=in_, d=ident: e.matmul(o, i, d, start=True,
                                                       stop=True)
        ev = self.emit("tensor", fn, deps, inc="pe")
        bank.set_w(ev)
        for r in reads:
            r.add_r(ev)
        if ident_res is not None:
            ident_res.add_r(ev)
        return ev

    def transpose(self, out, in_, ident, bank, reads=(), ident_res=None):
        deps = list(bank.for_write())
        for r in reads:
            deps += r.for_read()
        if ident_res is not None:
            deps += ident_res.for_read()
        fn = lambda e, o=out, i=in_, d=ident: e.transpose(o, i, d)
        ev = self.emit("tensor", fn, deps, inc="pe")
        bank.set_w(ev)
        for r in reads:
            r.add_r(ev)
        if ident_res is not None:
            ident_res.add_r(ev)
        return ev

    def op(self, eng, fn, reads=(), wres=None, wres2=None):
        wl = []
        if wres is not None:
            wl = list(wres) if isinstance(wres, list) else [wres]
        if wres2 is not None:
            wl.append(wres2)
        deps = []
        for w in wl:
            deps += w.for_write()
        for r in reads:
            deps += r.for_read()
        ev = self.emit(eng, fn, deps, inc=ESEM[eng])
        for w in wl:
            w.set_w(ev)
        for r in reads:
            r.add_r(ev)
        return ev


def ts(i, n):
    return slice(i * n, (i + 1) * n)


def build_nc(num_tiles=NT, dbg=False):
    nc = bass.Bass("TRN2", target_bir_lowering=False)
    skind = {"kind": "ExternalOutput"} if dbg else {}

    x_in = nc.dram_tensor("x", [T, D], F32, kind="ExternalInput").ap()
    wqT = nc.dram_tensor("wqT", [D, D], F32, kind="ExternalInput").ap()
    wkv_pm = nc.dram_tensor("wkv_pm", [2, 128, DC * 128], F32, kind="ExternalInput").ap()
    wo_blk = nc.dram_tensor("wo_blk", [8, 128, DC * 256], F32, kind="ExternalInput").ap()
    wg_blk = nc.dram_tensor("wg_blk", [E, 8, 128, DC * 256], F32, kind="ExternalInput").ap()
    wu_blk = nc.dram_tensor("wu_blk", [E, 8, 128, DC * 256], F32, kind="ExternalInput").ap()
    wd_blk = nc.dram_tensor("wd_blk", [E, 8, 128, FC * 256], F32, kind="ExternalInput").ap()
    ga_in = nc.dram_tensor("ga", [NL, D, R], F32, kind="ExternalInput").ap()
    gb_in = nc.dram_tensor("gb", [NL, R, F], F32, kind="ExternalInput").ap()
    ua_in = nc.dram_tensor("ua", [NL, D, R], F32, kind="ExternalInput").ap()
    ub_in = nc.dram_tensor("ub", [NL, R, F], F32, kind="ExternalInput").ap()
    da_in = nc.dram_tensor("da", [NL, F, R], F32, kind="ExternalInput").ap()
    db_in = nc.dram_tensor("db", [NL, R, D], F32, kind="ExternalInput").ap()
    wr_in = nc.dram_tensor("wr", [D, E], F32, kind="ExternalInput").ap()
    temp_in = nc.dram_tensor("temp", [1, 1], F32, kind="ExternalInput").ap()
    cc_identb = nc.dram_tensor("cc_identb", [128, 128], BF16, kind="ExternalInput").ap()
    cc_identf = nc.dram_tensor("cc_identf", [128, 128], F32, kind="ExternalInput").ap()
    cc_mask = nc.dram_tensor("cc_mask", [128, 128], F32, kind="ExternalInput").ap()
    cc_ropec = nc.dram_tensor("cc_ropec", [128, D], F32, kind="ExternalInput").ap()
    cc_ropes = nc.dram_tensor("cc_ropes", [128, D], F32, kind="ExternalInput").ap()
    cc_onesb = nc.dram_tensor("cc_onesb", [1, 128], BF16, kind="ExternalInput").ap()
    cc_onesf = nc.dram_tensor("cc_onesf", [1, 128], F32, kind="ExternalInput").ap()

    out_ap = nc.dram_tensor("out", [T, D], F32, kind="ExternalOutput").ap()
    if dbg:
        dbg_k = nc.dram_tensor("dbg_k", [128, T], BF16, kind="ExternalOutput").ap()
        dbg_v = nc.dram_tensor("dbg_v", [128, T], BF16, kind="ExternalOutput").ap()
        dbg_q = nc.dram_tensor("dbg_q", [128, TT], BF16, kind="ExternalOutput").ap()
        dbg_S = nc.dram_tensor("dbg_S", [128, 4096], F32, kind="ExternalOutput").ap()
        dbg_P = nc.dram_tensor("dbg_P", [128, 4096], BF16, kind="ExternalOutput").ap()
        dbg_att = nc.dram_tensor("dbg_att", [H, 128, TT], BF16, kind="ExternalOutput").ap()
        dbg_res = nc.dram_tensor("dbg_res", [128, D], F32, kind="ExternalOutput").ap()
        dbg_xfm = nc.dram_tensor("dbg_xfm", [128, DC * TT], BF16, kind="ExternalOutput").ap()
        dbg_rsum = nc.dram_tensor("dbg_rsum", [128, NQ], F32, kind="ExternalOutput").ap()
        dbg_bc = nc.dram_tensor("dbg_bc", [128, 256], F32, kind="ExternalOutput").ap()
        dbg_pt = nc.dram_tensor("dbg_pt", [4, 128, 256], BF16, kind="ExternalOutput").ap()

    s_xbf = nc.dram_tensor("s_xbf", [T, D], BF16, **skind).ap()
    s_y = nc.dram_tensor("s_y", [T, D], BF16, **skind).ap()
    s_wq = nc.dram_tensor("s_wq", [H, 128, DC * 128], BF16, **skind).ap()
    s_wo = nc.dram_tensor("s_wo", [8, 128, DC * 256], BF16).ap()
    s_wg = nc.dram_tensor("s_wg", [E, 8, 128, DC * 256], BF16).ap()
    s_wu = nc.dram_tensor("s_wu", [E, 8, 128, DC * 256], BF16).ap()
    s_wd = nc.dram_tensor("s_wd", [E, 8, 128, FC * 256], BF16).ap()
    s_wkv = nc.dram_tensor("s_wkv", [2, 128, DC * 128], BF16).ap()

    es = ExitStack()
    sb = lambda name, shape, dt: es.enter_context(nc.sbuf_tensor(name, shape, dt))
    psum = lambda name, shape, dt: es.enter_context(nc.psum_tensor(name, shape, dt))

    identb = sb("identb", [128, 128], BF16)
    identf = sb("identf", [128, 128], F32)
    mask_sb = sb("mask_sb", [128, 128], F32)
    onesb = sb("onesb", [1, 128], BF16)
    onesf = sb("onesf", [1, 128], F32)
    wr_sb = sb("wr_sb", [128, DC * E], BF16)
    nrt_sb = sb("nrt_sb", [128, 1], F32)
    t_tmp = sb("t_tmp", [1, 2], F32)
    ga_sb = sb("ga_sb", [128, NL * DC * R], BF16)
    ua_sb = sb("ua_sb", [128, NL * DC * R], BF16)
    da_sb = sb("da_sb", [128, NL * FC * R], BF16)
    gb_sb = sb("gb_sb", [128, F], BF16)
    ub_sb = sb("ub_sb", [128, F], BF16)
    db_sb = sb("db_sb", [128, D], BF16)
    k_fm = sb("k_fm", [128, T], BF16)
    v_tm = sb("v_tm", [128, T], BF16)
    wkv_sb = sb("wkv_sb", [128, 2 * DC * 128], BF16)
    xy_fm = sb("xy_fm", [128, DC * TT], BF16)
    arena_bf = sb("arena_bf", [128, 32768], BF16)
    arena_f = sb("arena_f", [128, 12288], F32)
    x_ring = sb("x_ring", [128, 2 * 512], F32)
    ybf_ring = sb("ybf_ring", [128, 2 * D], BF16)
    sg_ring = sb("sg_ring", [128, 2 * 512], BF16)
    t1_ring = sb("t1_ring", [128, 2 * 512], F32)
    wb_sb = sb("wb_sb", [128, 512], F32)
    bcast_sb = sb("bcast_sb", [128, 256], F32)
    a_fm = sb("a_fm", [128, TT], BF16)
    au_fm = sb("au_fm", [128, TT], BF16)
    am_fm = sb("am_fm", [128, TT], BF16)
    aev_sb = sb("aev_sb", [128, 64], BF16)
    srow_sb = sb("srow_sb", [1, 256], F32)
    rrow_sb = sb("rrow_sb", [1, 512], BF16)
    negmax_sb = sb("negmax_sb", [128, NQ], F32)
    rsum_sb = sb("rsum_sb", [128, NQ], F32)
    sm2_sb = sb("sm2_sb", [128, 2], F32)
    smax_sb = sb("smax_sb", [128, NQ * 8], F32)
    bnt_sb = sb("bnt_sb", [128, 24], F32)
    mv_sb = sb("mv_sb", [128, 2], F32)
    ln_sc = sb("ln_sc", [128, 4], F32)
    rt_sb = sb("rt_sb", [128, 12 * E], F32)
    rt1_sb = sb("rt1_sb", [128, 8], F32)
    w_tm = sb("w_tm", [128, NQ * E], F32)

    ps_mmA = psum("ps_mmA", [128, 512], F32)
    ps_mmB = psum("ps_mmB", [128, 512], F32)
    ps_mmC = psum("ps_mmC", [128, 512], F32)
    ps_pv = psum("ps_pv", [128, 512], F32)
    ps_tb1 = psum("ps_tb1", [128, 512], F32)
    ps_tb2 = psum("ps_tb2", [128, 512], F32)
    ps_tf = psum("ps_tf", [128, 512], F32)
    ps_bc = psum("ps_bc", [128, 512], F32)

    sems = {}
    for s in ALL_SEMS:
        sems[s] = es.enter_context(nc.semaphore("sem_" + s))
    kb = KB(sems)

    R_ = Res
    r_identb, r_identf, r_mask, r_onesb, r_onesf = R_(), R_(), R_(), R_(), R_()
    r_wr, r_nrt, r_ttmp = R_(), R_(), R_()
    r_ga, r_ua, r_da, r_gb, r_ub, r_db = R_(), R_(), R_(), R_(), R_(), R_()
    r_kfm, r_vtm, r_wkv = R_(), R_(), R_()
    r_xyfm = R_()
    r_xring = [R_() for _ in range(2)]
    r_ybf = [R_() for _ in range(2)]
    r_sg = [R_() for _ in range(2)]
    r_t1 = [R_() for _ in range(2)]
    r_wb, r_bcast = R_(), R_()
    r_afm, r_aufm, r_amfm, r_aev = R_(), R_(), R_(), R_()
    r_srow, r_rrow = R_(), R_()
    r_negmax, r_rsum, r_sm2, r_smax = R_(), R_(), R_(), R_()
    r_bnt, r_mv, r_lnsc = R_(), R_(), R_()
    r_rt, r_rt1, r_wtm = R_(), R_(), R_()
    r_psA, r_psB, r_psC, r_psPV = R_(), R_(), R_(), R_()
    r_psT1, r_psT2, r_psTF, r_psBC = R_(), R_(), R_(), R_()
    r_sxbf = [R_() for _ in range(32)]
    r_sy = [R_() for _ in range(32)]
    r_swq, r_swo, r_swkv = R_(), R_(), R_()
    r_swg = [R_() for _ in range(E)]
    r_swu = [R_() for _ in range(E)]
    r_swd = [R_() for _ in range(E)]
    r_out = R_()

    A_PT = lambda kc: arena_bf[:, kc * 256:(kc + 1) * 256]
    A_PSB = arena_bf[:, 8192:12288]
    A_WQ = lambda j: arena_bf[:, 12288 + j * 2048:12288 + (j + 1) * 2048]
    A_QBF = arena_bf[:, 16384:16896]
    A_ATT = lambda h: arena_bf[:, 17408 + h * 512:17408 + (h + 1) * 512]
    A_WO = lambda j: arena_bf[:, j * 4096:(j + 1) * 4096]
    A_WGU = lambda j: arena_bf[:, j * 4096:(j + 1) * 4096]
    A_WD = lambda j: arena_bf[:, 16384 + j * 4096:16384 + (j + 1) * 4096]
    A_MID = lambda fc: arena_bf[:, 24576 + fc * 512:24576 + (fc + 1) * 512]
    r_PT = [R_() for _ in range(32)]
    r_psb, r_qbf = R_(), R_()
    r_wq = [R_() for _ in range(2)]
    r_att = [R_() for _ in range(H)]
    r_wo = [R_() for _ in range(4)]
    r_wgu = [R_() for _ in range(4)]
    r_wd = [R_() for _ in range(2)]
    r_mid = [R_() for _ in range(FC)]

    A_ACC = lambda dc: arena_f[:, dc * 512:(dc + 1) * 512]
    A_RES4 = lambda qs: arena_f[:, qs * 2048:(qs + 1) * 2048]
    A_S = arena_f[:, 8192:12288]
    A_SROW = lambda r: arena_f[:, r * 4096:(r + 1) * 4096]
    A_PROW = (arena_bf[:, 8192:12288], arena_bf[:, 25600:29696])
    A_RES = lambda j: arena_f[:, 8192 + j * 2048:8192 + (j + 1) * 2048]
    P_CC = arena_f[:, 0:2048]
    P_SS = arena_f[:, 2048:4096]
    P_STG = arena_f[:, 4096:6144]
    P_ROT = arena_f[:, 6144:8192]
    r_acc = [R_() for _ in range(DC)]
    r_res4 = [R_() for _ in range(NQ)]
    r_S = R_()
    r_Sb = [R_() for _ in range(8)]
    r_Srow = [[R_() for _ in range(8)] for _ in range(3)]
    r_Prow = [R_(), R_()]
    r_res = [R_() for _ in range(2)]
    r_pcc, r_pss, r_pstg, r_prot, r_ptmp = R_(), R_(), R_(), R_(), R_()
    r_wqstg = R_()

    with es:
        kb.dma("sync", identb[:], cc_identb, wres=r_identb)
        kb.dma("sync", identf[:], cc_identf, wres=r_identf)
        kb.dma("sync", mask_sb[:], cc_mask, wres=r_mask)
        kb.dma("sync", onesb[:], cc_onesb, wres=r_onesb)
        kb.dma("sync", onesf[:], cc_onesf, wres=r_onesf)
        kb.dma("sync", P_CC, cc_ropec, wres=r_pcc)
        kb.dma("sync", P_SS, cc_ropes, wres=r_pss)
        kb.dma("sync", t_tmp[0:1, 0:1], temp_in, wres=r_ttmp)
        kb.op("vector", lambda e: e.reciprocal(out=t_tmp[0:1, 1:2], in_=t_tmp[0:1, 0:1]),
              reads=[r_ttmp], wres=r_ttmp)
        kb.op("vector", lambda e: e.tensor_scalar(out=t_tmp[0:1, 1:2],
                                                  in0=t_tmp[0:1, 1:2],
                                                  scalar1=-1.0, scalar2=None,
                                                  op0=ALU.mult),
              reads=[r_ttmp], wres=r_ttmp)
        kb.mmgroup([(ps_bc[:, 0:1], onesf[:], t_tmp[0:1, 1:2])],
                   r_psBC, reads=[r_onesf, r_ttmp])
        kb.op("scalar", lambda e: e.copy(out=nrt_sb[:], in_=ps_bc[:, 0:1]),
              reads=[r_psBC], wres=r_nrt)

        kb.dma("gpsimd", s_wkv.rearrange("a p d -> (a p) d"),
               wkv_pm.rearrange("a p d -> (a p) d"), wres=r_swkv, sem="q_ctw")
        kb.dma("sync", wkv_sb[:, 0:DC * 128], s_wkv[0], reads=[r_swkv],
               wres=r_wkv, sem="q_wkv")
        kb.dma("sync", wkv_sb[:, DC * 128:2 * DC * 128], s_wkv[1],
               reads=[r_swkv], wres=r_wkv, sem="q_wkv")
        for st in range(4 * num_tiles):
            sl = st % 2
            for half in range(2):
                xsl = half
                kb.dma("scalar", x_ring[:, ts(xsl, 512)],
                       x_in[ts(st, 128), ts(2 * half, 512)],
                       wres=r_xring[xsl], sem=f"q_xr{xsl}")
                kb.op("scalar", lambda e, sl=sl, half=half, xsl=xsl: e.copy(
                    out=ybf_ring[:, sl * D + 1024 * half: sl * D + 1024 * half + 512],
                    in_=x_ring[:, ts(xsl, 512)]),
                    reads=[r_xring[xsl]], wres=r_ybf[sl])
                kb.dma("scalar", x_ring[:, ts(xsl, 512)],
                       x_in[ts(st, 128), ts(2 * half + 1, 512)],
                       wres=r_xring[xsl], sem=f"q_xr{xsl}")
                kb.op("scalar", lambda e, sl=sl, half=half, xsl=xsl: e.copy(
                    out=ybf_ring[:, sl * D + 1024 * half + 512:
                                sl * D + 1024 * half + 1024],
                    in_=x_ring[:, ts(xsl, 512)]),
                    reads=[r_xring[xsl]], wres=r_ybf[sl])
            kb.dma("scalar", s_xbf[ts(st, 128), :], ybf_ring[:, ts(sl, D)],
                   reads=[r_ybf[sl]], wres=r_sxbf[st], sem="q_sxb")
            if st == 3:
                for dc in range(DC):
                    kb.dma("scalar" if dc % 2 == 0 else "sync",
                           xy_fm[:, ts(dc, TT)], s_xbf[0:TT, ts(dc, 128)],
                           reads=[r_sxbf[j] for j in range(4)],
                           wres=r_xyfm, tp=True, sem="q_xy")

        AROW = 12288
        ap_tmp = bass.AP(arena_f, 8192, [[AROW, 128], [1, 2048]])
        ap_te = bass.AP(arena_f, 8192, [[AROW, 128], [2, 1024]])
        ap_to = bass.AP(arena_f, 8193, [[AROW, 128], [2, 1024]])
        ap_se = bass.AP(arena_f, 4096, [[AROW, 128], [2, 1024]])
        ap_so = bass.AP(arena_f, 4097, [[AROW, 128], [2, 1024]])
        ap_ce = bass.AP(arena_f, 2048, [[AROW, 128], [2, 1024]])
        ap_co = bass.AP(arena_f, 2049, [[AROW, 128], [2, 1024]])
        for dc in range(DC):
            kb.dma("sync", P_STG, wqT[ts(dc, 128), :], wres=r_pstg,
                   sem="q_stg")
            kb.op("vector", lambda e: e.tensor_tensor(out=P_ROT, in0=P_STG,
                                                      in1=P_CC, op=ALU.mult),
                  reads=[r_pstg, r_pcc], wres=r_prot)
            kb.op("vector", lambda e: e.tensor_tensor(
                out=ap_te, in0=ap_so, in1=ap_ce, op=ALU.mult),
                reads=[r_pstg, r_pss], wres=r_ptmp)
            kb.op("vector", lambda e: e.tensor_tensor(
                out=ap_to, in0=ap_se, in1=ap_co, op=ALU.mult),
                reads=[r_pstg, r_pss], wres=r_ptmp)
            kb.op("vector", lambda e: e.tensor_tensor(
                out=arena_bf[:, 0:2048], in0=P_ROT, in1=ap_tmp, op=ALU.add),
                reads=[r_prot, r_ptmp], wres=r_wqstg)
            for half in range(2):
                    dst = bass.AP(s_wq.tensor,
                                  (half * 8) * (128 * DC * 128) + dc * 128,
                                  [[DC * 128, 128], [128 * DC * 128, 8], [1, 128]])
                    srcv = bass.AP(arena_bf, half * 8 * 128,
                                   [[32768, 128], [128, 8], [1, 128]])
                    kb.dma("sync" if half == 0 else "gpsimd", dst, srcv,
                           reads=[r_wqstg], wres=r_swq, sem="q_swq")

        for l in range(NL):
            for dc in range(DC):
                kb.dma("gpsimd", ga_sb[:, ts(l * DC + dc, R)],
                       ga_in[l, ts(dc, 128), :], wres=r_ga, sem="q_ctm")
                kb.dma("gpsimd", ua_sb[:, ts(l * DC + dc, R)],
                       ua_in[l, ts(dc, 128), :], wres=r_ua, sem="q_ctm")
                kb.dma("gpsimd", da_sb[:, ts(l * DC + dc, R)],
                       da_in[l, ts(dc, 128), :], wres=r_da, sem="q_ctm")
            kb.dma("gpsimd", gb_sb[64 * l:64 * l + 64, :], gb_in[l], wres=r_gb, sem="q_ctm")
            kb.dma("gpsimd", ub_sb[64 * l:64 * l + 64, :], ub_in[l], wres=r_ub, sem="q_ctm")
            kb.dma("gpsimd", db_sb[64 * l:64 * l + 64, :], db_in[l], wres=r_db, sem="q_ctm")
        for dc in range(DC):
            kb.dma("gpsimd", wr_sb[:, ts(dc, E)], wr_in[ts(dc, 128), :], wres=r_wr, sem="q_ctm")
        kb.dma("gpsimd", s_wo.rearrange("a p d -> (a p) d"),
               wo_blk.rearrange("a p d -> (a p) d"), wres=r_swo, sem="q_ctm")

        boundary_res = [r_pcc, r_pss, r_pstg, r_prot, r_ptmp, r_wqstg,
                        r_ybf[0], r_ybf[1], r_xring[0], r_xring[1], r_t1[0]]

        for t in range(num_tiles):
            tile_rows = ts(t, TT)

            if t == 0:
                seed(r_PT + [r_psb, r_qbf, r_S] + r_Sb + r_Srow[0]
                     + r_Srow[1] + r_Srow[2] + r_Prow + r_wq + r_att + r_acc
                     + r_res, boundary_res)
            else:
                seed(r_PT + [r_psb, r_qbf] + r_wq + r_att, r_wgu + r_wd + r_mid)
                seed([r_S] + r_Sb + r_Srow[0] + r_Srow[1] + r_Srow[2],
                     r_res + r_acc + r_res4)
                seed(r_Prow, r_mid)

            if t > 0:
                for dc in range(DC):
                    kb.dma("scalar" if dc % 2 == 0 else "sync",
                           xy_fm[:, ts(dc, TT)],
                           s_xbf[tile_rows, ts(dc, 128)],
                           reads=[r_sxbf[4 * t + j] for j in range(4)],
                           wres=r_xyfm, tp=True, sem="q_xy")

            kb.mmgroup([(ps_mmA[:], wkv_sb[:, ts(dc, 128)], xy_fm[:, ts(dc, TT)])
                        for dc in range(DC)], r_psA, reads=[r_wkv, r_xyfm])
            kb.op("scalar", lambda e, t=t: e.copy(out=k_fm[:, ts(t, TT)],
                                                  in_=ps_mmA[:]),
                  reads=[r_psA], wres=r_kfm)
            for qs in range(NQ):
                kb.mmgroup([(ps_mmB[:, 0:128],
                             xy_fm[:, dc * TT + qs * 128:dc * TT + (qs + 1) * 128],
                             wkv_sb[:, ts(DC + dc, 128)]) for dc in range(DC)],
                           r_psB, reads=[r_wkv, r_xyfm])
                kb.op("scalar", lambda e, t=t, qs=qs: e.copy(
                    out=v_tm[:, ts(4 * t + qs, 128)], in_=ps_mmB[:, 0:128]),
                    reads=[r_psB], wres=r_vtm)

            if dbg and t == 0:
                kb.dma("gpsimd", dbg_k, k_fm[:], reads=[r_kfm], sem="q_dbg")
                kb.dma("gpsimd", dbg_v, v_tm[:], reads=[r_vtm], sem="q_dbg")
                kb.dma("gpsimd", dbg_xfm, xy_fm[:], reads=[r_xyfm], sem="q_dbg")

            for h in range(H):
                if t == 0 and h % 2 == 0:
                    e_ = h // 2
                    gate = [("pe", kb.cnt["pe"])]
                    for smat, dmat, rres in ((s_wg, wg_blk, r_swg[e_]),
                                             (s_wu, wu_blk, r_swu[e_]),
                                             (s_wd, wd_blk, r_swd[e_])):
                        deps = list(gate)
                        fn = (lambda e, o=smat[e_].rearrange("a p d -> (a p) d"),
                              i=dmat[e_].rearrange("a p d -> (a p) d"):
                              e.dma_start(out=o, in_=i))
                        ev = kb.emit("gpsimd", fn, deps, inc=f"q_ct{e_}",
                                     inc_n=16)
                        rres.set_w(ev)
                wq_slot = h % 2
                kb.dma("sync", A_WQ(wq_slot), s_wq[h], reads=[r_swq],
                       wres=r_wq[wq_slot], sem=f"q_wq{wq_slot}")
                kb.mmgroup([(ps_mmA[:], A_WQ(wq_slot)[:, ts(dc, 128)],
                             xy_fm[:, ts(dc, TT)]) for dc in range(DC)],
                           r_psA, reads=[r_wq[wq_slot], r_xyfm])
                kb.op("scalar", lambda e: e.copy(out=A_QBF, in_=ps_mmA[:]),
                      reads=[r_psA], wres=r_qbf)
                def emit_S(qs):
                    qi = 4 * t + qs
                    W = (qi + 1) * 128
                    row = qs % 3
                    for kb_i in range(t + 1):
                        w0 = kb_i * 512
                        wN = min(512, W - w0)
                        bank, rbank = [(ps_mmB, r_psB), (ps_mmC, r_psC),
                                       (ps_mmA, r_psA)][kb_i % 3]
                        kb.mmgroup([(bank[:, 0:wN], A_QBF[:, ts(qs, 128)],
                                     k_fm[:, w0:w0 + wN])],
                                   rbank, reads=[r_qbf, r_kfm])
                        rs_ = r_Srow[row][kb_i]
                        if kb_i < t:
                            kb.op("vector", lambda e, w0=w0, wN=wN, bank=bank,
                                  row=row, qs=qs, kb_i=kb_i: e.tensor_scalar(
                                out=A_SROW(row)[:, w0:w0 + wN],
                                in0=bank[:, 0:wN], scalar1=0.0, scalar2=None,
                                op0=ALU.add, op1=ALU.max,
                                accum_out=smax_sb[:, qs * 8 + kb_i:
                                                  qs * 8 + kb_i + 1]),
                                reads=[rbank], wres=rs_, wres2=r_smax)
                        else:
                            if wN > 128:
                                kb.op("vector", lambda e, w0=w0, wN=wN, bank=bank,
                                      row=row: e.tensor_copy(
                                    out=A_SROW(row)[:, w0:w0 + wN - 128],
                                    in_=bank[:, 0:wN - 128]),
                                    reads=[rbank], wres=rs_)
                            kb.op("vector", lambda e, w0=w0, wN=wN, bank=bank,
                                  row=row: e.tensor_tensor(
                                out=A_SROW(row)[:, w0 + wN - 128:w0 + wN],
                                in0=bank[:, wN - 128:wN], in1=mask_sb[:],
                                op=ALU.add), reads=[rbank, r_mask], wres=rs_)
                            kb.op("vector", lambda e, w0=w0, wN=wN, row=row,
                                  qs=qs, kb_i=kb_i: e.reduce_max(
                                out=smax_sb[:, qs * 8 + kb_i:qs * 8 + kb_i + 1],
                                in_=A_SROW(row)[:, w0:w0 + wN], axis=AX.X),
                                reads=[rs_], wres=r_smax)
                    kb.op("vector", lambda e, qs=qs, nb=t + 1: e.reduce_max(
                        out=negmax_sb[:, qs:qs + 1],
                        in_=smax_sb[:, qs * 8:qs * 8 + nb], axis=AX.X,
                        negate=True), reads=[r_smax], wres=r_negmax)

                def emit_T(qs):
                    qi = 4 * t + qs
                    W = (qi + 1) * 128
                    row = qs % 3
                    prow = qs % 2
                    hs = (qs % 2) * 128
                    nblk = t + 1
                    for bb in range((nblk + 1) // 2):
                        w0 = bb * 1024
                        wN = min(1024, W - w0)
                        rds = [r_Srow[row][2 * bb], r_negmax]
                        if 2 * bb + 1 < nblk:
                            rds.append(r_Srow[row][2 * bb + 1])
                        kb.op("scalar", lambda e, w0=w0, wN=wN, qs=qs,
                              row=row, prow=prow:
                              e.activation(out=A_PROW[prow][:, w0:w0 + wN],
                                           in_=A_SROW(row)[:, w0:w0 + wN],
                                           func=AF.Exp,
                                           bias=negmax_sb[:, qs:qs + 1],
                                           scale=1.0, accum_out=sm2_sb[:, 1:2]),
                              reads=rds, wres=r_Prow[prow], wres2=r_sm2)
                        if bb == 0:
                            kb.op("vector", lambda e, qs=qs: e.tensor_copy(
                                out=rsum_sb[:, qs:qs + 1], in_=sm2_sb[:, 1:2]),
                                reads=[r_sm2], wres=r_rsum)
                        else:
                            kb.op("vector", lambda e, qs=qs: e.tensor_tensor(
                                out=rsum_sb[:, qs:qs + 1],
                                in0=rsum_sb[:, qs:qs + 1],
                                in1=sm2_sb[:, 1:2], op=ALU.add),
                                reads=[r_sm2, r_rsum], wres=r_rsum)
                        for half8 in range(2):
                            kc0 = bb * 8 + half8 * 4
                            if kc0 > qi:
                                break
                            nk4 = min(qi + 1, kc0 + 4) - kc0
                            tb, rtb = ((ps_tb1, r_psT1) if half8 == 0
                                       else (ps_tb2, r_psT2))
                            for j in range(nk4):
                                kb.tmm(tb[:, ts(j, 128)],
                                       A_PROW[prow][:, ts(kc0 + j, 128)],
                                       identb[:], rtb, reads=[r_Prow[prow]],
                                       ident_res=r_identb)
                            pt_dst = bass.AP(arena_bf, kc0 * 256 + hs,
                                             [[32768, 128], [256, nk4], [1, 128]])
                            wl = [r_PT[kc0 + j] for j in range(nk4)]
                            if half8 == 0:
                                kb.op("scalar", lambda e, d=pt_dst, tb=tb,
                                      nk4=nk4: e.copy(out=d,
                                                      in_=tb[:, 0:nk4 * 128]),
                                      reads=[rtb], wres=wl)
                            else:
                                kb.op("vector", lambda e, d=pt_dst, tb=tb,
                                      nk4=nk4: e.tensor_copy(
                                    out=d, in_=tb[:, 0:nk4 * 128]),
                                    reads=[rtb], wres=wl)
                    if qs % 2 == 1:
                        kb.op("vector", lambda e, qi=qi: e.memset(
                            A_PT(qi)[:, 0:128], 0.0), wres=r_PT[qi])

                def emit_PV(Hh):
                    qs_hi = 2 * Hh + 1
                    nk = 4 * t + qs_hi + 1
                    kb.mmgroup([(ps_pv[:, 0:256], v_tm[:, ts(kc, 128)],
                                 A_PT(kc)[:, 0:256]) for kc in range(nk)],
                               r_psPV,
                               reads=[r_vtm] + [r_PT[kc] for kc in range(nk)])

                def emit_norm_a(Hh):
                    for j in range(2):
                        qsj = 2 * Hh + j
                        kb.transpose(ps_tf[0:1, ts(j, 128)],
                                     rsum_sb[:, qsj:qsj + 1], identf[:],
                                     r_psTF, reads=[r_rsum], ident_res=r_identf)
                    kb.op("scalar", lambda e: e.copy(out=srow_sb[0:1, 0:256],
                                                     in_=ps_tf[0:1, 0:256]),
                          reads=[r_psTF], wres=r_srow)
                    kb.op("vector", lambda e: e.reciprocal(
                        out=rrow_sb[0:1, 0:256], in_=srow_sb[0:1, 0:256]),
                        reads=[r_srow], wres=r_rrow)

                def emit_norm_b(Hh, h):
                    kb.mmgroup([(ps_bc[:, 0:256], onesb[:], rrow_sb[0:1, 0:256])],
                               r_psBC, reads=[r_onesb, r_rrow])
                    kb.op("scalar", lambda e: e.copy(out=bcast_sb[:],
                                                     in_=ps_bc[:, 0:256]),
                          reads=[r_psBC], wres=r_bcast)
                    kb.op("vector", lambda e, h=h, Hh=Hh: e.tensor_tensor(
                        out=A_ATT(h)[:, ts(Hh, 256)], in0=ps_pv[:, 0:256],
                        in1=bcast_sb[:], op=ALU.mult),
                        reads=[r_psPV, r_bcast], wres=r_att[h])

                emit_S(0)
                emit_S(1)
                emit_S(2)
                emit_T(0)
                emit_S(3)
                emit_T(1)
                emit_PV(0)
                emit_norm_a(0)
                emit_T(2)
                emit_norm_b(0, h)
                emit_T(3)
                emit_norm_a(1)
                emit_PV(1)
                emit_norm_b(1, h)

            if dbg and t == 0:
                for h in range(H):
                    kb.dma("gpsimd", dbg_att[h], A_ATT(h), reads=[r_att[h]], sem="q_dbg")

            seed(r_wo, r_PT + [r_psb] + r_wq + [r_qbf])
            seed(r_res4, r_acc + r_Srow[0] + r_Srow[1])
            seed(r_res, r_Srow[2])
            seed(r_res, [r_S])

            def ln_pass(res_ap, res_res, out_dst, out_wres):
                kb.op("vector", lambda e, res_ap=res_ap: e.bn_stats(
                    out=bnt_sb[:, 0:6], in_=res_ap[:, 0:512]),
                    reads=[res_res], wres=r_bnt)
                for kq in range(1, 4):
                    kb.op("vector", lambda e, kq=kq, res_ap=res_ap: e.bn_stats(
                        out=bnt_sb[:, ts(kq, 6)], in_=res_ap[:, ts(kq, 512)]),
                        reads=[res_res], wres=r_bnt)
                kb.op("vector", lambda e: e.bn_aggr(out=mv_sb[:], in_=bnt_sb[:]),
                      reads=[r_bnt], wres=r_mv)
                kb.op("vector", lambda e: e.tensor_scalar(
                    out=ln_sc[:, 0:1], in0=mv_sb[:, 1:2], scalar1=EPS, scalar2=None,
                    op0=ALU.add), reads=[r_mv], wres=r_lnsc)
                kb.op("scalar", lambda e: e.activation(
                    out=ln_sc[:, 3:4], in_=ln_sc[:, 0:1], func=AF.Ln),
                    reads=[r_lnsc], wres=r_lnsc)
                kb.op("vector", lambda e: e.tensor_scalar(
                    out=ln_sc[:, 3:4], in0=ln_sc[:, 3:4], scalar1=-0.5, scalar2=None,
                    op0=ALU.mult), reads=[r_lnsc], wres=r_lnsc)
                kb.op("scalar", lambda e: e.activation(
                    out=ln_sc[:, 1:2], in_=ln_sc[:, 3:4], func=AF.Exp),
                    reads=[r_lnsc], wres=r_lnsc)
                kb.op("vector", lambda e: e.scalar_tensor_tensor(
                    out=ln_sc[:, 2:3], in0=mv_sb[:, 0:1], scalar=-1.0,
                    in1=ln_sc[:, 1:2], op0=ALU.mult, op1=ALU.mult),
                    reads=[r_mv, r_lnsc], wres=r_lnsc)
                kb.op("scalar", lambda e, res_ap=res_ap, out_dst=out_dst:
                      e.activation(out=out_dst, in_=res_ap[:],
                                   func=AF.Identity, scale=ln_sc[:, 1:2],
                                   bias=ln_sc[:, 2:3]),
                      reads=[res_res, r_lnsc], wres=out_wres)

            for db8 in range(8):
                wo_slot = db8 % 4
                kb.dma("sync", A_WO(wo_slot), s_wo[db8], reads=[r_swo],
                       wres=r_wo[wo_slot], sem=f"q_wo{wo_slot}")
                for qs in range(NQ):
                    bank, rbank = ((ps_mmA, r_psA) if qs % 2 == 0
                                   else (ps_mmB, r_psB))
                    kb.mmgroup([(bank[:, 0:256], A_ATT(c)[:, ts(qs, 128)],
                                 A_WO(wo_slot)[:, ts(c, 256)]) for c in range(DC)],
                               rbank,
                               reads=[r_wo[wo_slot]] + [r_att[c] for c in range(DC)])
                    kb.op("vector", lambda e, db8=db8, bank=bank, qs=qs:
                          e.tensor_copy(out=A_RES4(qs)[:, ts(db8, 256)],
                                        in_=bank[:, 0:256]),
                          reads=[rbank], wres=r_res4[qs])
            for qs in range(NQ):
                xs = qs % 2
                kb.dma("scalar", A_RES(xs)[:], x_in[ts(4 * t + qs, 128), :],
                       reads=[], wres=r_res[xs], sem=f"q_xr{xs}")
                kb.op("vector", lambda e, qs=qs, xs=xs: e.tensor_tensor(
                    out=A_RES4(qs)[:], in0=A_RES4(qs), in1=A_RES(xs)[:],
                    op=ALU.add), reads=[r_res[xs], r_res4[qs]], wres=r_res4[qs])
                if dbg and t == 0 and qs == 0:
                    kb.dma("gpsimd", dbg_res, A_RES4(qs)[:],
                           reads=[r_res4[qs]], sem="q_dbg")
                ydst = ybf_ring[:, ts(qs % 2, D)]
                ln_pass(A_RES4(qs), r_res4[qs], ydst, r_ybf[qs % 2])
                kb.dma("gpsimd", s_y[ts(4 * t + qs, 128), :], ydst,
                       reads=[r_ybf[qs % 2]], wres=r_sy[4 * t + qs],
                       sem="q_sy")
                for g4 in range(4):
                    tb, rtb = ((ps_tb1, r_psT1) if g4 % 2 == 0
                               else (ps_tb2, r_psT2))
                    for j in range(4):
                        dc = g4 * 4 + j
                        kb.tmm(tb[:, ts(j, 128)],
                               ybf_ring[:, (qs % 2) * D + dc * 128:
                                        (qs % 2) * D + (dc + 1) * 128],
                               identb[:], rtb, reads=[r_ybf[qs % 2]],
                               ident_res=r_identb)
                    y_dst = bass.AP(xy_fm, (g4 * 4) * TT + qs * 128,
                                    [[DC * TT, 128], [TT, 4], [1, 128]])
                    if g4 % 2 == 0:
                        kb.op("scalar", lambda e, d=y_dst, tb=tb:
                              e.copy(out=d, in_=tb[:, 0:512]),
                              reads=[rtb], wres=r_xyfm)
                    else:
                        kb.op("vector", lambda e, d=y_dst, tb=tb:
                              e.tensor_copy(out=d, in_=tb[:, 0:512]),
                              reads=[rtb], wres=r_xyfm)


            for qs in range(NQ):
                kb.mmgroup([(ps_pv[:, 0:8],
                             xy_fm[:, dc * TT + qs * 128:dc * TT + (qs + 1) * 128],
                             wr_sb[:, ts(dc, 8)]) for dc in range(DC)],
                           r_psPV, reads=[r_xyfm, r_wr])
                RT = lambda j: rt_sb[:, ts(j, 8)]
                S1 = lambda j: rt1_sb[:, j:j + 1]
                kb.op("vector", lambda e: e.tensor_copy(out=RT(0), in_=ps_pv[:, 0:8]),
                      reads=[r_psPV], wres=r_rt)
                kb.op("scalar", lambda e: e.activation(out=RT(1), in_=RT(0),
                                                       func=AF.Exp,
                                                       scale=nrt_sb[:, 0:1]),
                      reads=[r_rt, r_nrt], wres=r_rt)
                kb.op("vector", lambda e: e.tensor_scalar(
                    out=RT(1), in0=RT(1), scalar1=1.0, scalar2=None, op0=ALU.add),
                    reads=[r_rt], wres=r_rt)
                kb.op("vector", lambda e: e.reciprocal(out=RT(2), in_=RT(1)),
                      reads=[r_rt], wres=r_rt)
                kb.op("vector", lambda e: e.tensor_scalar(
                    out=RT(3), in0=RT(2), scalar1=GOLDEN_LOWER, scalar2=None,
                    op0=ALU.is_ge), reads=[r_rt], wres=r_rt)
                kb.op("vector", lambda e: e.tensor_scalar(
                    out=RT(4), in0=RT(2), scalar1=GOLDEN_UPPER, scalar2=None,
                    op0=ALU.is_le), reads=[r_rt], wres=r_rt)
                kb.op("vector", lambda e: e.tensor_tensor(
                    out=RT(3), in0=RT(3), in1=RT(4), op=ALU.mult),
                    reads=[r_rt], wres=r_rt)
                kb.op("vector", lambda e: e.tensor_scalar(
                    out=RT(5), in0=RT(2), scalar1=GOLDEN_CENTER, scalar2=None,
                    op0=ALU.subtract), reads=[r_rt], wres=r_rt)
                kb.op("scalar", lambda e: e.activation(
                    out=RT(5), in_=RT(5), func=AF.Abs), reads=[r_rt], wres=r_rt)
                kb.op("vector", lambda e: e.tensor_scalar(
                    out=RT(6), in0=RT(5), scalar1=-10.0, scalar2=None, op0=ALU.mult),
                    reads=[r_rt], wres=r_rt)
                kb.op("scalar", lambda e: e.activation(out=RT(6), in_=RT(6),
                                                       func=AF.Exp),
                      reads=[r_rt], wres=r_rt)
                kb.op("vector", lambda e: e.tensor_tensor(
                    out=RT(6), in0=RT(6), in1=RT(3), op=ALU.mult),
                    reads=[r_rt], wres=r_rt)
                kb.op("vector", lambda e: e.reduce_sum(out=S1(0), in_=RT(6),
                                                       axis=AX.X),
                      reads=[r_rt], wres=r_rt1)
                kb.op("vector", lambda e: e.tensor_scalar(
                    out=S1(1), in0=S1(0), scalar1=1e-8, scalar2=None, op0=ALU.is_lt),
                    reads=[r_rt1], wres=r_rt1)
                kb.op("vector", lambda e: e.tensor_scalar(
                    out=S1(2), in0=S1(1), scalar1=-1.0, scalar2=1.0,
                    op0=ALU.mult, op1=ALU.add), reads=[r_rt1], wres=r_rt1)
                kb.op("vector", lambda e: e.tensor_scalar(
                    out=RT(7), in0=RT(5), scalar1=-1.0 / 0.3, scalar2=None,
                    op0=ALU.mult), reads=[r_rt], wres=r_rt)
                kb.op("scalar", lambda e: e.activation(out=RT(7), in_=RT(7),
                                                       func=AF.Exp),
                      reads=[r_rt], wres=r_rt)
                kb.op("vector", lambda e: e.reduce_max(out=S1(3), in_=RT(7),
                                                       axis=AX.X),
                      reads=[r_rt], wres=r_rt1)
                kb.op("vector", lambda e: e.tensor_scalar(
                    out=RT(8), in0=RT(7), scalar1=S1(3), scalar2=None, op0=ALU.is_ge),
                    reads=[r_rt, r_rt1], wres=r_rt)
                kb.op("vector", lambda e: e.tensor_scalar(
                    out=RT(9), in0=RT(8), scalar1=-1.0, scalar2=1.0,
                    op0=ALU.mult, op1=ALU.add), reads=[r_rt], wres=r_rt)
                kb.op("vector", lambda e: e.tensor_tensor(
                    out=RT(9), in0=RT(7), in1=RT(9), op=ALU.mult),
                    reads=[r_rt], wres=r_rt)
                kb.op("vector", lambda e: e.reduce_max(out=S1(4), in_=RT(9),
                                                       axis=AX.X),
                      reads=[r_rt], wres=r_rt1)
                kb.op("vector", lambda e: e.tensor_scalar(
                    out=RT(10), in0=RT(9), scalar1=S1(4), scalar2=None, op0=ALU.is_ge),
                    reads=[r_rt, r_rt1], wres=r_rt)
                kb.op("vector", lambda e: e.tensor_tensor(
                    out=RT(10), in0=RT(10), in1=RT(8), op=ALU.add),
                    reads=[r_rt], wres=r_rt)
                kb.op("vector", lambda e: e.tensor_tensor(
                    out=RT(10), in0=RT(10), in1=RT(7), op=ALU.mult),
                    reads=[r_rt], wres=r_rt)
                kb.op("vector", lambda e: e.reduce_sum(out=S1(5), in_=RT(10),
                                                       axis=AX.X),
                      reads=[r_rt], wres=r_rt1)
                kb.op("vector", lambda e: e.tensor_scalar(
                    out=S1(5), in0=S1(5), scalar1=1e-8, scalar2=None, op0=ALU.max),
                    reads=[r_rt1], wres=r_rt1)
                kb.op("vector", lambda e: e.reciprocal(out=S1(6), in_=S1(5)),
                      reads=[r_rt1], wres=r_rt1)
                kb.op("vector", lambda e: e.tensor_scalar(
                    out=RT(10), in0=RT(10), scalar1=S1(6), scalar2=S1(1),
                    op0=ALU.mult, op1=ALU.mult), reads=[r_rt, r_rt1], wres=r_rt)
                kb.op("vector", lambda e: e.tensor_scalar(
                    out=RT(11), in0=RT(6), scalar1=S1(2), scalar2=None, op0=ALU.mult),
                    reads=[r_rt, r_rt1], wres=r_rt)
                kb.op("vector", lambda e: e.tensor_tensor(
                    out=RT(11), in0=RT(11), in1=RT(10), op=ALU.add),
                    reads=[r_rt], wres=r_rt)
                kb.op("vector", lambda e: e.reduce_sum(out=S1(7), in_=RT(11),
                                                       axis=AX.X),
                      reads=[r_rt], wres=r_rt1)
                kb.op("vector", lambda e: e.tensor_scalar(
                    out=S1(7), in0=S1(7), scalar1=1e-8, scalar2=None, op0=ALU.max),
                    reads=[r_rt1], wres=r_rt1)
                kb.op("vector", lambda e: e.reciprocal(out=S1(0), in_=S1(7)),
                      reads=[r_rt1], wres=r_rt1)
                kb.op("vector", lambda e, qs=qs: e.tensor_scalar(
                    out=w_tm[:, ts(qs, E)], in0=RT(11), scalar1=S1(0), scalar2=None,
                    op0=ALU.mult), reads=[r_rt, r_rt1], wres=r_wtm)

            seed(r_wgu + r_wd + r_mid,
                 r_wo + r_att + [r_qbf, r_psb] + r_PT + r_wq + r_Prow)
            seed(r_acc, r_res4)
            for e_ in range(E):
                for qs in range(NQ):
                    kb.transpose(ps_tf[0:1, 256:384],
                                 w_tm[:, qs * E + e_:qs * E + e_ + 1], identf[:],
                                 r_psTF, reads=[r_wtm], ident_res=r_identf)
                    kb.op("scalar", lambda e, qs=qs: e.copy(
                        out=rrow_sb[0:1, ts(qs, 128)], in_=ps_tf[0:1, 256:384]),
                        reads=[r_psTF], wres=r_rrow)
                kb.mmgroup([(ps_bc[:], onesb[:], rrow_sb[0:1, 0:512])],
                           r_psBC, reads=[r_onesb, r_rrow])
                kb.op("scalar", lambda e: e.copy(out=wb_sb[:], in_=ps_bc[:]),
                      reads=[r_psBC], wres=r_wb)

                if e_ < NL:
                    l = e_
                    for nm in ("g", "u"):
                        asrc = ga_sb if nm == "g" else ua_sb
                        ares_src = r_ga if nm == "g" else r_ua
                        adst = a_fm if nm == "g" else au_fm
                        ares = r_afm if nm == "g" else r_aufm
                        for qs in range(NQ):
                            kb.mmgroup([(ps_pv[:, 0:64],
                                         xy_fm[:, dc * TT + qs * 128:
                                               dc * TT + (qs + 1) * 128],
                                         asrc[:, ts(l * DC + dc, R)])
                                        for dc in range(DC)], r_psPV,
                                       reads=[r_xyfm, ares_src])
                            kb.op("scalar", lambda e: e.copy(out=aev_sb[:, 0:64],
                                                             in_=ps_pv[:, 0:64]),
                                  reads=[r_psPV], wres=r_aev)
                            kb.tmm(ps_tb1[0:64, 0:128], aev_sb[:, 0:64],
                                   identb[:], r_psT1, reads=[r_aev],
                                   ident_res=r_identb)
                            kb.op("scalar", lambda e, qs=qs, adst=adst: e.copy(
                                out=adst[0:64, ts(qs, 128)], in_=ps_tb1[0:64, 0:128]),
                                reads=[r_psT1], wres=ares)
                            kb.op("scalar", lambda e, qs=qs, adst=adst: e.copy(
                                out=adst[64:128, ts(qs, 128)], in_=ps_tb1[0:64, 0:128]),
                                reads=[r_psT1], wres=ares)

                for fb in range(8):
                    gu_g = fb % 2
                    gu_u = 2 + fb % 2
                    kb.dma("sync", A_WGU(gu_g), s_wg[e_, fb], reads=[r_swg[e_]],
                           wres=r_wgu[gu_g], sem=f"q_gu{gu_g}")
                    kb.dma("sync", A_WGU(gu_u), s_wu[e_, fb], reads=[r_swu[e_]],
                           wres=r_wgu[gu_u], sem=f"q_gu{gu_u}")
                    for fs in range(2):
                        fc = 2 * fb + fs
                        gbank, rgbank = ((ps_mmA, r_psA) if fc % 2 == 0
                                         else (ps_tf, r_psTF))
                        mms = [(gbank[:], A_WGU(gu_g)[:, dc * 256 + fs * 128:
                                                      dc * 256 + fs * 128 + 128],
                                xy_fm[:, ts(dc, TT)]) for dc in range(DC)]
                        rds = [r_wgu[gu_g], r_xyfm]
                        if e_ < NL:
                            mms.append((gbank[:],
                                        gb_sb[64 * e_:64 * e_ + 64,
                                              fc * 128:(fc + 1) * 128],
                                        a_fm[64 * e_:64 * e_ + 64, :]))
                            rds += [r_gb, r_afm]
                        kb.mmgroup(mms, rgbank, reads=rds)
                        sgs = fc % 2
                        kb.op("scalar", lambda e, sgs=sgs, gbank=gbank:
                              e.activation(out=sg_ring[:, ts(sgs, 512)],
                                           in_=gbank[:], func=AF.Silu),
                              reads=[rgbank], wres=r_sg[sgs])
                        ubank, rubank = ((ps_mmB, r_psB) if fc % 2 == 0
                                         else (ps_bc, r_psBC))
                        mms = [(ubank[:], A_WGU(gu_u)[:, dc * 256 + fs * 128:
                                                      dc * 256 + fs * 128 + 128],
                                xy_fm[:, ts(dc, TT)]) for dc in range(DC)]
                        rds = [r_wgu[gu_u], r_xyfm]
                        if e_ < NL:
                            mms.append((ubank[:],
                                        ub_sb[64 * e_:64 * e_ + 64,
                                              fc * 128:(fc + 1) * 128],
                                        au_fm[64 * e_:64 * e_ + 64, :]))
                            rds += [r_ub, r_aufm]
                        kb.mmgroup(mms, rubank, reads=rds)
                        kb.op("vector", lambda e, fc=fc, sgs=sgs, ubank=ubank:
                              e.tensor_tensor(out=A_MID(fc), in0=ubank[:],
                                              in1=sg_ring[:, ts(sgs, 512)],
                                              op=ALU.mult),
                              reads=[rubank, r_sg[sgs]], wres=r_mid[fc])

                if e_ < NL:
                    l = e_
                    kb.mmgroup([(ps_pv[0:64, :],
                                 da_sb[:, ts(l * FC + fc, R)], A_MID(fc))
                                for fc in range(FC)], r_psPV,
                               reads=[r_da] + [r_mid[fc] for fc in range(FC)])
                    kb.op("scalar", lambda e: e.copy(out=am_fm[0:64, :],
                                                     in_=ps_pv[0:64, :]),
                          reads=[r_psPV], wres=r_amfm)
                    kb.op("scalar", lambda e: e.copy(out=am_fm[64:128, :],
                                                     in_=ps_pv[0:64, :]),
                          reads=[r_psPV], wres=r_amfm)

                for db2 in range(8):
                    wd_slot = db2 % 2
                    kb.dma("sync", A_WD(wd_slot), s_wd[e_, db2], reads=[r_swd[e_]],
                           wres=r_wd[wd_slot], sem=f"q_wd{wd_slot}")
                    for ds in range(2):
                        dchunk = 2 * db2 + ds
                        dbank, rdbank = ((ps_mmC, r_psC) if dchunk % 2 == 0
                                         else (ps_pv, r_psPV))
                        mms = [(dbank[:],
                                A_WD(wd_slot)[:, fc * 256 + ds * 128:
                                              fc * 256 + ds * 128 + 128],
                                A_MID(fc)) for fc in range(FC)]
                        rds = [r_wd[wd_slot]] + [r_mid[fc] for fc in range(FC)]
                        if e_ < NL:
                            mms.append((dbank[:],
                                        db_sb[64 * e_:64 * e_ + 64,
                                              dchunk * 128:(dchunk + 1) * 128],
                                        am_fm[64 * e_:64 * e_ + 64, :]))
                            rds += [r_db, r_amfm]
                        kb.mmgroup(mms, rdbank, reads=rds)
                        if e_ == 0:
                            kb.op("vector", lambda e, dchunk=dchunk, dbank=dbank:
                                  e.tensor_tensor(out=A_ACC(dchunk), in0=dbank[:],
                                                  in1=wb_sb[:], op=ALU.mult),
                                  reads=[rdbank, r_wb], wres=r_acc[dchunk])
                        else:
                            t1s = dchunk % 2
                            kb.op("vector", lambda e, dbank=dbank, t1s=t1s:
                                  e.tensor_tensor(out=t1_ring[:, ts(t1s, 512)],
                                                  in0=dbank[:], in1=wb_sb[:],
                                                  op=ALU.mult),
                                  reads=[rdbank, r_wb], wres=r_t1[t1s])
                            kb.op("vector", lambda e, dchunk=dchunk, t1s=t1s:
                                  e.tensor_tensor(out=A_ACC(dchunk),
                                                  in0=A_ACC(dchunk),
                                                  in1=t1_ring[:, ts(t1s, 512)],
                                                  op=ALU.add),
                                  reads=[r_t1[t1s], r_acc[dchunk]],
                                  wres=r_acc[dchunk])

            for qs in range(NQ):
                y2s = qs % 2
                kb.dma("scalar", ybf_ring[:, ts(y2s, D)], s_y[ts(4 * t + qs, 128), :],
                       reads=[r_sy[4 * t + qs]], wres=r_ybf[y2s],
                       sem=f"q_yb{y2s}")
                res_slot = qs % 2
                for dc in range(DC):
                    kb.transpose(ps_tf[:, ts(dc % 4, 128)],
                                 A_ACC(dc)[:, ts(qs, 128)], identf[:],
                                 r_psTF, reads=[r_acc[dc]], ident_res=r_identf)
                    kb.op("vector", lambda e, dc=dc, y2s=y2s, res_slot=res_slot:
                          e.tensor_tensor(
                        out=A_RES(res_slot)[:, ts(dc, 128)],
                        in0=ps_tf[:, ts(dc % 4, 128)],
                        in1=ybf_ring[:, y2s * D + dc * 128:y2s * D + (dc + 1) * 128],
                        op=ALU.add),
                        reads=[r_psTF, r_ybf[y2s]], wres=r_res[res_slot])
                out_slot = (qs + 1) % 2
                ln_pass(A_RES(res_slot), r_res[res_slot], A_RES(out_slot)[:],
                        r_res[out_slot])
                kb.dma("gpsimd", out_ap[ts(4 * t + qs, 128), :],
                       A_RES(out_slot)[:], reads=[r_res[out_slot]], wres=r_out,
                       sem="q_out")

        kb.emit("gpsimd", lambda e: None,
                [(s, kb.cnt[s]) for s in DMA_SEMS if kb.cnt[s] > 0])

        block = es.enter_context(nc.Block())

        def runner(eng):
            def run(engobj):
                with nc.allow_low_precision(
                        reason="bf16 matmul operands with fp32 accumulation are "
                               "the intended precision policy for this kernel"):
                    for fn, waits, inc, inc_n in kb.ops[eng]:
                        for s, v in waits:
                            engobj.wait_ge(sems[s], v)
                        ins = fn(engobj)
                        if inc is not None and ins is not None:
                            ins.then_inc(sems[inc], inc_n)
            return run

        block.tensor(runner("tensor"))
        block.vector(runner("vector"))
        block.scalar(runner("scalar"))
        block.sync(runner("sync"))
        block.gpsimd(runner("gpsimd"))

    return nc


def _consts():
    ident = np.eye(128, dtype=np.float32)
    mask = np.where(np.tril(np.ones((128, 128), dtype=bool)), 0.0,
                    -1e9).astype(np.float32)
    theta = 1.0 / 10000.0 ** (np.arange(0, 32, dtype=np.float64) / 32.0)
    cc = np.zeros((D,), dtype=np.float64)
    ss = np.zeros((D,), dtype=np.float64)
    for h in range(H):
        for p in range(64):
            ang = h * theta[p % 32]
            cc[128 * h + 2 * p] = math.cos(ang)
            cc[128 * h + 2 * p + 1] = math.cos(ang)
            ss[128 * h + 2 * p] = -math.sin(ang)
            ss[128 * h + 2 * p + 1] = math.sin(ang)
    cc *= SCALING
    ss *= SCALING
    return {
        "cc_identb": ident.astype(ml_dtypes.bfloat16),
        "cc_identf": ident,
        "cc_mask": mask,
        "cc_ropec": np.broadcast_to(cc.astype(np.float32), (128, D)).copy(),
        "cc_ropes": np.broadcast_to(ss.astype(np.float32), (128, D)).copy(),
        "cc_onesb": np.ones((1, 128), dtype=np.float32).astype(ml_dtypes.bfloat16),
        "cc_onesf": np.ones((1, 128), dtype=np.float32),
    }


def _prep_weights(i):
    c = np.ascontiguousarray
    f32 = lambda a: np.asarray(a, dtype=np.float32)
    wq = f32(i["L0_attn_Wq"])
    wk = f32(i["L0_attn_Wk"])
    wv = f32(i["L0_attn_Wv"])
    wo = f32(i["L0_attn_Wo"])
    wg = f32(i["L0_ffn_Wg"])
    wu = f32(i["L0_ffn_Wu"])
    wd = f32(i["L0_ffn_Wd"])
    wkv_pm = np.stack([
        wk.T.reshape(16, 128, 128).transpose(1, 0, 2).reshape(128, DC * 128),
        wv.T.reshape(16, 128, 128).transpose(1, 0, 2).reshape(128, DC * 128)])
    wo_b = wo.T.reshape(16, 128, 8, 256).transpose(2, 1, 0, 3).reshape(8, 128, DC * 256)
    blk = lambda w: (w.reshape(8, 16, 128, 8, 256).transpose(0, 3, 2, 1, 4)
                     .reshape(8, 8, 128, 16 * 256))
    return {
        "wqT": c(wq.T),
        "wkv_pm": c(wkv_pm),
        "wo_blk": c(wo_b),
        "wg_blk": c(blk(wg)),
        "wu_blk": c(blk(wu)),
        "wd_blk": c(blk(wd)),
        "ga": f32(i["L0_ffn_Ga"]),
        "gb": f32(i["L0_ffn_Gb"]),
        "ua": f32(i["L0_ffn_Ua"]),
        "ub": f32(i["L0_ffn_Ub"]),
        "da": f32(i["L0_ffn_Da"]),
        "db": f32(i["L0_ffn_Db"]),
        "wr": f32(i["L0_ffn_Wr"]),
        "temp": f32(i["L0_ffn_temperature"]).reshape(1, 1),
    }


def kernel(**inputs):
    x = np.asarray(inputs["L0_attn_hidden_states"], dtype=np.float32)
    wm = _prep_weights(inputs)
    wm.update(_consts())
    nc = build_nc()
    in_maps = []
    for b in range(B):
        m = dict(wm)
        m["x"] = np.ascontiguousarray(x[b])
        in_maps.append(m)
    res = run_bass_kernel_spmd(nc, in_maps, core_ids=list(range(B)))
    return np.stack([r["out"] for r in res.results], axis=0)
